# version 07
# baseline (speedup 1.0000x reference)
import contextlib
import numpy as np
import concourse.bass as bass
import concourse.mybir as mybir
from concourse.bass_utils import run_bass_kernel_spmd

F32 = mybir.dt.float32
BF16 = mybir.dt.bfloat16
I32 = mybir.dt.int32
AF = mybir.ActivationFunctionType
ALU = mybir.AluOpType

NCORE = 8
B_LOC = 2
T = 2048
NT = B_LOC * T
H2 = T
D = 1024
NCH = D // 128
TTILE = 512
NTT = NT // TTILE
CRND = 1.5 * 2 ** 23
LN_EPS = 1e-5
GN_EPS = 0.00064

ENGINES = ("sp", "pe", "act", "dve", "pool")


class Prog:

    def __init__(self, name="prog"):
        import os as _os
        self.nc = bass.Bass("TRN2", target_bir_lowering=False, dynamic_dma_scratch_size=512,
                            detect_race_conditions=bool(_os.environ.get("BASS_RACE")))
        self.es = contextlib.ExitStack()
        self.streams = {e: [] for e in ENGINES}
        self.eng_attr = {"sp": "sync", "pe": "tensor", "act": "scalar",
                         "dve": "vector", "pool": "gpsimd"}
        self.sems = {}
        self.cnt = {}
        for e in ENGINES:
            self._mk_sem("m_" + e)
            self._mk_sem("q_" + e)
        self.inputs = {}
        self.outputs = {}
        self._last_op = {e: False for e in ENGINES}
        self._finalized = False

    def _mk_sem(self, name):
        self.sems[name] = self.es.enter_context(self.nc.semaphore(name))
        self.cnt[name] = 0

    def inp(self, name, shape, dtype=F32):
        ap = self.nc.dram_tensor(name, list(shape), dtype, kind="ExternalInput").ap()
        self.inputs[name] = ap
        return ap

    def outp(self, name, shape, dtype=F32):
        ap = self.nc.dram_tensor(name, list(shape), dtype, kind="ExternalOutput").ap()
        self.outputs[name] = ap
        return ap

    def dram(self, name, shape, dtype=F32):
        return self.nc.dram_tensor(name, list(shape), dtype, kind="Internal").ap()

    def sbuf(self, name, shape, dtype=F32):
        return self.es.enter_context(self.nc.sbuf_tensor(name, list(shape), dtype))

    def psum(self, name, shape, dtype=F32):
        return self.es.enter_context(self.nc.psum_tensor(name, list(shape), dtype))

    def raw(self, eng, fn):
        self.streams[eng].append(fn)
        self._last_op[eng] = True

    def step(self, eng, fn=None, n=1):
        sem = self.sems["m_" + eng]
        self.cnt["m_" + eng] += n
        use_drain = eng in ("act", "pool", "dve", "pe")
        if fn is not None:
            if use_drain:
                def _f(E, fn=fn, sem=sem, n=n):
                    fn(E)
                    E.drain().then_inc(sem, n)
            else:
                def _f(E, fn=fn, sem=sem, n=n):
                    fn(E).then_inc(sem, n)
            self.streams[eng].append(_f)
            self._last_op[eng] = True
        elif self._last_op[eng]:
            g = self.streams[eng].pop()
            if use_drain:
                def _f(E, g=g, sem=sem, n=n):
                    g(E)
                    E.drain().then_inc(sem, n)
            else:
                def _f(E, g=g, sem=sem, n=n):
                    g(E).then_inc(sem, n)
            self.streams[eng].append(_f)
            self._last_op[eng] = False
        else:
            def _f(E, sem=sem, n=n):
                E.nop().then_inc(sem, n)
            self.streams[eng].append(_f)
        return self.cnt["m_" + eng]

    def wait(self, eng, sem_name, count):
        if count <= 0:
            return
        sem = self.sems[sem_name]
        self.streams[eng].append(lambda E, s=sem, c=count: E.wait_ge(s, c))
        self._last_op[eng] = False

    def wait_eng(self, eng, other, count):
        self.wait(eng, "m_" + other, count)

    def dma(self, eng, out, in_, **kw):
        sem = self.sems["q_" + eng]
        self.cnt["q_" + eng] += 16
        def _f(E, out=out, in_=in_, sem=sem, kw=kw):
            E.dma_start(out=out, in_=in_, **kw).then_inc(sem, 16)
        self.streams[eng].append(_f)
        self._last_op[eng] = False
        return self.cnt["q_" + eng]

    def snap(self):
        return dict(self.cnt)

    def barrier(self, engines=ENGINES):
        snap = self.snap()
        for e in engines:
            self.step(e)
            self.wait(e, "q_" + e, snap["q_" + e])
            self.step(e)
        snap2 = self.snap()
        for e in engines:
            for f in engines:
                if f != e:
                    self.wait(e, "m_" + f, snap2["m_" + f])

    def finalize(self):
        assert not self._finalized
        self._finalized = True
        for e in ENGINES:
            self.wait(e, "q_" + e, self.cnt["q_" + e])
        block = self.es.enter_context(self.nc.Block())
        for e in ENGINES:
            stream = self.streams[e]
            def body(E, stream=stream):
                for fn in stream:
                    fn(E)
            getattr(block, self.eng_attr[e])(body)
        return self.nc



def ts2(p, eng, out, in0, s1, s2, op0, op1):
    p.raw(eng, lambda E: E.tensor_scalar(out=out, in0=in0, scalar1=s1, scalar2=s2,
                                         op0=op0, op1=op1))

def ts1(p, eng, out, in0, s1, op0):
    p.raw(eng, lambda E: E.tensor_scalar(out=out, in0=in0, scalar1=s1, scalar2=None,
                                         op0=op0))

def tt(p, eng, out, in0, in1, op):
    p.raw(eng, lambda E: E.tensor_tensor(out=out, in0=in0, in1=in1, op=op))

def stt(p, eng, out, in0, scalar, in1, op0, op1):
    p.raw(eng, lambda E: E.scalar_tensor_tensor(out=out, in0=in0, scalar=scalar,
                                                in1=in1, op0=op0, op1=op1))

def act(p, out, in_, func, bias=None, scale=None, accum_out=None):
    kw = {}
    if bias is not None:
        kw["bias"] = bias
    if scale is not None:
        kw["scale"] = scale
    if accum_out is not None:
        kw["accum_out"] = accum_out
    p.raw("act", lambda E: E.activation(out=out, in_=in_, func=func, **kw))

def copy(p, eng, out, in_):
    p.raw(eng, lambda E: E.tensor_copy(out=out, in_=in_))

def recip(p, out, in_):
    p.raw("dve", lambda E: E.reciprocal(out=out, in_=in_))

def mm(p, out, lhsT, rhs, start, stop):
    p.raw("pe", lambda E: E.matmul(out=out, lhsT=lhsT, rhs=rhs, start=start, stop=stop))



def emit_ln(p, x_in, xn, ar, ones128, label):
    F1, F2, SM, SCR, QT, ps = ar["F1"], ar["F2"], ar["SM"], ar["SCR"], ar["QT"], ar["ps"]
    cvec = ar["cvec"]
    for h in range(2):
        pe_done = 0
        for c in range(NCH):
            if c >= 2:
                p.wait("sp", "m_pe", pe_done)
            else:
                p.wait("sp", "m_dve", p.cnt["m_dve"])
            ld = p.dma("sp", F1[:, c % 2, :], x_in[c * 128:(c + 1) * 128, h * H2:(h + 1) * H2])
            p.wait("dve", "q_sp", ld)
            copy(p, "dve", xn[:, c, h * H2:(h + 1) * H2], F1[:, c % 2, :])
            dv = p.step("dve")
            for q in range(4):
                sl = slice(h * H2 + q * TTILE, h * H2 + (q + 1) * TTILE)
                p.wait_eng("act", "dve", dv)
                p.wait_eng("act", "pe", pe_done)
                act(p, QT[:, (q % 2) * TTILE:(q % 2 + 1) * TTILE], xn[:, c, sl], AF.Square)
                av = p.step("act")
                p.wait_eng("pe", "dve", dv)
                p.wait_eng("pe", "act", av)
                mm(p, ps[q][:], ones128[:], xn[:, c, sl], start=(c == 0), stop=(c == NCH - 1))
                mm(p, ps[4 + q][:], ones128[:], QT[:, (q % 2) * TTILE:(q % 2 + 1) * TTILE],
                   start=(c == 0), stop=(c == NCH - 1))
                pe_done = p.step("pe")
        p.wait_eng("act", "pe", pe_done)
        for q in range(4):
            qsl = slice(q * TTILE, (q + 1) * TTILE)
            mu, e2, m2 = SM[:, 0:TTILE], SM[:, TTILE:2 * TTILE], SCR[:, 0:TTILE]
            p.wait_eng("act", "dve", p.cnt["m_dve"])
            act(p, mu, ps[q][:], AF.Copy, scale=1.0 / D, bias=0.0)
            act(p, e2, ps[4 + q][:], AF.Copy, scale=1.0 / D, bias=0.0)
            av = p.step("act")
            p.wait_eng("dve", "act", av)
            tt(p, "dve", m2, mu, mu, ALU.mult)
            tt(p, "dve", e2, e2, m2, ALU.subtract)
            dv = p.step("dve")
            p.wait_eng("act", "dve", dv)
            act(p, m2, e2, AF.Sqrt, bias=cvec[:, 0:1])
            av = p.step("act")
            p.wait_eng("dve", "act", av)
            recip(p, F2[:, 0, qsl], m2)
            tt(p, "dve", F2[:, 1, qsl], mu, F2[:, 0, qsl], ALU.mult)
            ts1(p, "dve", F2[:, 1, qsl], F2[:, 1, qsl], -1.0, ALU.mult)
            p.step("dve")
        dv = p.cnt["m_dve"]
        for c in range(NCH):
            p.wait("sp", "m_dve", dv)
            ld = p.dma("sp", F1[:, c % 2, :], x_in[c * 128:(c + 1) * 128, h * H2:(h + 1) * H2])
            p.wait("dve", "q_sp", ld)
            sl = slice(h * H2, (h + 1) * H2)
            tt(p, "dve", xn[:, c, sl], F1[:, c % 2, :], F2[:, 0, :], ALU.mult)
            tt(p, "dve", xn[:, c, sl], xn[:, c, sl], F2[:, 1, :], ALU.add)
            dv = p.step("dve")
    p.barrier()



def emit_l0(p, x_in, x_out, w, stop_after=None):
    nc = p.nc

    xn = p.sbuf("l0_xn", [128, NCH, NT], BF16)
    ab = p.sbuf("l0_ab", [128, NCH, NT], BF16)
    F1 = p.sbuf("l0_f1", [128, 2, H2], F32)
    F2 = p.sbuf("l0_f2", [128, 2, H2], F32)
    F3 = p.sbuf("l0_f3", [128, NT], F32)
    AH = p.sbuf("l0_ah", [128, 8192], BF16)
    AX = p.sbuf("l0_ax", [128, 4096], BF16)
    W32 = p.sbuf("l0_w32", [128, 1024], F32)
    SCR = p.sbuf("l0_scr", [128, 1024], F32)
    SM = p.sbuf("l0_sm", [128, 1024], F32)
    QT = p.sbuf("l0_qt", [128, 1024], BF16)
    W1S = p.sbuf("l0_w1s", [128, 2048], BF16)
    mixc = p.sbuf("l0_mixs", [128, NCH, 6], F32)
    bcol = p.sbuf("l0_bcol", [128, 64], F32)
    bd64 = p.sbuf("l0_bd64", [128, 128], BF16)
    ones128 = p.sbuf("l0_ones", [128, 128], BF16)
    cvec = p.sbuf("l0_cvec", [128, 8], F32)
    crt = p.sbuf("l0_crt", [128, 256], F32)
    ps = [p.psum(f"l0_ps{i}", [128, 512], F32) for i in range(8)]
    ar = {"F1": F1, "F2": F2, "SM": SM, "SCR": SCR, "QT": QT, "ps": ps, "cvec": cvec}

    dec_d = p.dram("l0_dec", [NCH, 128, NT], F32)
    r_d = p.dram("l0_r", [NCH, 128, NT], BF16)
    k_d = p.dram("l0_k", [NCH, 128, NT], BF16)
    v_d = p.dram("l0_v", [NCH, 128, NT], BF16)
    a_d = p.dram("l0_a", [NCH, 128, NT], BF16)
    g_d = p.dram("l0_g", [NCH, 128, NT], BF16)
    xacc_d = p.dram("l0_xacc", [D, NT], F32)
    w1q_d = p.dram("l0_w1q", [32, 128, NCH, 128], BF16)
    w2q_d = p.dram("l0_w2q", [NCH, 128, 32, 128], BF16)

    p.dma("sp", cvec[:], w["cvec"])
    p.raw("pool", lambda E: E.memset(crt[:], CRND))
    p.step("pool")
    p.dma("sp", mixc[:], w["l0_mix"])
    p.dma("sp", bd64[:], w["bd64"])
    p.dma("sp", ones128[:], w["ones128"])
    p.dma("sp", bcol[:, 0:8], w["l0_bout"])
    p.dma("sp", bcol[:, 8:16], w["l0_gnw"])
    cl = p.dma("sp", bcol[:, 16:24], w["l0_gnb"])
    p.wait("dve", "q_sp", cl)
    p.wait("pe", "q_sp", cl)
    p.wait("act", "q_sp", cl)
    p.wait("pool", "q_sp", cl)

    emit_ln(p, x_in, xn, ar, ones128, "l0")
    if stop_after == "p0":
        xob = x_out.bitcast(BF16)
        for c in range(NCH):
            p.dma("act", xob[c * 128:(c + 1) * 128, 0:NT], xn[:, c, :])
        return

    proj_dst = {0: r_d, 1: dec_d, 2: k_d, 3: v_d, 4: a_d, 5: g_d}
    proj_fun = {0: (AF.Sigmoid, 1.0), 1: (AF.Sigmoid, -1.0), 2: (AF.Tanh, 1.0),
                3: (AF.Copy, 1.0), 4: (AF.Sigmoid, 1.0), 5: (AF.Sigmoid, 1.0)}
    WT6 = w["l0_W6T"]
    wfull = AH[:].rearrange("p (i m) -> p i m", i=NCH)
    xm8 = AX[:].rearrange("p (i t) -> p i t", i=NCH)
    og_acts = {}
    for j in range(6):
        fn, scl = proj_fun[j]
        for i in range(NCH):
            p.wait("sp", "m_dve", p.cnt["m_dve"])
            ld = p.dma("sp", W32[:], WT6[j, i * 128:(i + 1) * 128, :])
            p.wait("dve", "q_sp", ld)
            p.wait_eng("dve", "pe", p.cnt["m_pe"])
            copy(p, "dve", wfull[:, i, :], W32[:])
            p.step("dve")
        wready = p.cnt["m_dve"]
        for tti in range(NTT):
            tok = slice(tti * TTILE, (tti + 1) * TTILE)
            xm_counts = {}
            pe_now = p.cnt["m_pe"]
            for i in range(NCH):
                eng = "dve" if i % 2 == 0 else "pool"
                p.wait_eng(eng, "pe", pe_now)
                if tok.start in (0, H2):
                    ts1(p, eng, xm8[:, i, 0:1], xn[:, i, tok.start:tok.start + 1], -1.0, ALU.mult)
                    tt(p, eng, xm8[:, i, 1:TTILE],
                       xn[:, i, tok.start:tok.stop - 1], xn[:, i, tok.start + 1:tok.stop],
                       ALU.subtract)
                else:
                    tt(p, eng, xm8[:, i, :],
                       xn[:, i, tok.start - 1:tok.stop - 1], xn[:, i, tok], ALU.subtract)
                if eng == "dve":
                    stt(p, eng, xm8[:, i, :], xm8[:, i, :], mixc[:, i, j:j + 1],
                        xn[:, i, tok], ALU.mult, ALU.add)
                else:
                    ts1(p, eng, xm8[:, i, :], xm8[:, i, :], mixc[:, i, j:j + 1], ALU.mult)
                    tt(p, eng, xm8[:, i, :], xm8[:, i, :], xn[:, i, tok], ALU.add)
                xm_counts[i] = p.step(eng)
            for og in range(2):
                pset = ps[og * 4:(og + 1) * 4]
                p.wait_eng("pe", "act", og_acts.get((j, tti - 1, og), og_acts.get("last", 0)))
                p.wait_eng("pe", "dve", wready)
                for i in range(NCH):
                    p.wait_eng("pe", "dve" if i % 2 == 0 else "pool", xm_counts[i])
                for o in range(4):
                    oc = og * 4 + o
                    for i in range(NCH):
                        mm(p, pset[o][:], wfull[:, i, oc * 128:(oc + 1) * 128],
                           xm8[:, i, :], start=(i == 0), stop=(i == NCH - 1))
                pec = p.step("pe")
                p.wait_eng("act", "pe", pec)
                for o in range(4):
                    oc = og * 4 + o
                    if j == 1:
                        dst = SCR[:, (o % 2) * 512:(o % 2 + 1) * 512]
                        p.wait("act", "q_act", max(0, p.cnt["q_act"] - 16))
                        act(p, dst, pset[o][:], fn, scale=scl)
                        av = p.step("act")
                        p.dma("act", dec_d[oc, :, tok], dst)
                    else:
                        dst = QT[:, (o % 2) * 512:(o % 2 + 1) * 512]
                        p.wait("act", "q_act", max(0, p.cnt["q_act"] - 16))
                        act(p, dst, pset[o][:], fn, scale=scl)
                        av = p.step("act")
                        p.dma("act", proj_dst[j][oc, :, tok], dst)
                og_acts[(j, tti, og)] = p.cnt["m_act"]
                og_acts["last"] = p.cnt["m_act"]
    p.barrier()
    if stop_after == "p1a":
        xob = x_out.bitcast(BF16)
        for c in range(NCH):
            p.dma("act", xob[c * 128:(c + 1) * 128, 0:NT], xn[:, c, :])
        return

    kF3 = F3[:].bitcast(BF16)
    for c in range(NCH):
        p.wait("sp", "m_dve", p.cnt["m_dve"])
        l_dec = p.dma("sp", F1[:].rearrange("p a b -> p (a b)"), dec_d[c])
        l_k = p.dma("sp", kF3[:, 0:NT], k_d[c])
        l_v = p.dma("sp", kF3[:, NT:2 * NT], v_d[c])
        l_a = p.dma("sp", AX[:], a_d[c])
        p.wait("dve", "q_sp", l_a)
        uF2 = F2[:].rearrange("p a b -> p (a b)")
        tt(p, "dve", uF2, AX[:], kF3[:, 0:NT], ALU.mult)
        tt(p, "dve", uF2, uF2, kF3[:, NT:2 * NT], ALU.mult)
        dv = p.step("dve")
        if stop_after == "p1bu":
            p.barrier()
            p.dma("act", x_out[0:128, 0:NT], uF2)
            return
        decF1 = F1[:].rearrange("p a b -> p (a b)")
        for b in range(B_LOC):
            sl = slice(b * T, (b + 1) * T)
            p.raw("dve", lambda E, o=F3[:, sl], d=decF1[:, sl], u=uF2[:, sl]:
                  E.tensor_tensor_scan(out=o, data0=d, data1=u, initial=0.0,
                                       op0=ALU.mult, op1=ALU.add))
        dv = p.step("dve")
        if stop_after == "p1bs":
            p.barrier()
            p.dma("act", x_out[0:128, 0:NT], F3[:])
            return
        p.wait("sp", "m_dve", dv)
        l_r = p.dma("sp", AX[:], r_d[c])
        p.wait("dve", "q_sp", l_r)
        tt(p, "dve", F3[:], F3[:], AX[:], ALU.mult)
        dv = p.step("dve")
        p.wait("sp", "m_dve", dv)
        l_g = p.dma("sp", AX[:], g_d[c])
        for tti in range(NTT):
            tok = slice(tti * TTILE, (tti + 1) * TTILE)
            mb = QT[:, 0:512]
            sqb = QT[:, 512:1024]
            p.wait_eng("dve", "pe", p.cnt["m_pe"])
            copy(p, "dve", mb, F3[:, tok])
            dvc = p.step("dve")
            p.wait_eng("act", "dve", dvc)
            act(p, sqb, mb, AF.Square)
            avc = p.step("act")
            p.wait_eng("pe", "dve", dvc)
            p.wait_eng("pe", "act", avc)
            p.wait_eng("pe", "dve", dv)
            mm(p, ps[6][:], bd64[:], mb, start=True, stop=True)
            mm(p, ps[7][:], bd64[:], sqb, start=True, stop=True)
            pec = p.step("pe")
            if stop_after == "p1bg0":
                p.barrier()
                act(p, F3[:, 0:512], ps[6][:], AF.Copy, scale=1.0, bias=0.0)
                act(p, F3[:, 512:1024], ps[7][:], AF.Copy, scale=1.0, bias=0.0)
                p.step("act")
                p.dma("act", x_out[0:128, 0:1024], F3[:, 0:1024])
                return
            mu, e2 = SM[:, 0:512], SM[:, 512:1024]
            m2 = SCR[:, 0:512]
            B = SCR[:, 512:1024]
            if stop_after == "p1bg0b3":
                p.barrier()
            elif stop_after == "p1bgA":
                p.wait_eng("act", "pe", pec)
            elif stop_after == "p1bgB":
                p.wait_eng("act", "pe", pec)
                p.wait_eng("act", "dve", p.cnt["m_dve"])
                p.raw("act", lambda E: E.nop())
                p.raw("act", lambda E: E.nop())
            else:
                p.wait_eng("act", "pe", pec)
                p.wait_eng("act", "dve", p.cnt["m_dve"])
            act(p, mu, ps[6][:], AF.Copy, scale=1.0 / 64, bias=0.0)
            act(p, e2, ps[7][:], AF.Copy, scale=1.0 / 64, bias=0.0)
            avc = p.step("act")
            if stop_after in ("p1bgA", "p1bgB"):
                p.barrier()
                p.dma("act", x_out[0:128, 0:512], SM[:, 0:512])
                return
            if stop_after and stop_after.startswith("p1bg0b"):
                p.barrier()
                p.dma("act", x_out[0:128, 0:512], SM[:, 0:512])
                return
            p.wait_eng("dve", "act", avc)
            tt(p, "dve", m2, mu, mu, ALU.mult)
            tt(p, "dve", e2, e2, m2, ALU.subtract)
            dvc = p.step("dve")
            if stop_after == "p1bg0c":
                p.barrier()
                p.dma("act", x_out[0:128, 0:512], SM[:, 0:512])
                return
            p.wait_eng("act", "dve", dvc)
            act(p, m2, e2, AF.Sqrt, bias=cvec[:, 1:2])
            avc = p.step("act")
            p.wait_eng("dve", "act", avc)
            if stop_after == "p1bg1":
                p.barrier()
                p.dma("act", x_out[0:128, 0:512], SM[:, 0:512])
                p.dma("act", x_out[0:128, 512:1024], SM[:, 512:1024])
                p.dma("act", x_out[0:128, 1024:1536], SCR[:, 0:512])
                return
            recip(p, e2, m2)
            ts1(p, "dve", e2, e2, bcol[:, 8 + c:9 + c], ALU.mult)
            tt(p, "dve", B, mu, e2, ALU.mult)
            ts2(p, "dve", B, B, -1.0, bcol[:, 16 + c:17 + c], ALU.mult, ALU.add)
            tt(p, "dve", mu, F3[:, tok], e2, ALU.mult)
            tt(p, "dve", mu, mu, B, ALU.add)
            if tti == 0:
                p.wait("dve", "q_sp", l_g)
            tt(p, "dve", ab[:, c, tok], mu, AX[:, tok], ALU.mult)
            p.step("dve")
        if stop_after == "p1bg":
            p.barrier()
            xob = x_out.bitcast(BF16)
            p.dma("act", xob[0:128, 0:NT], ab[:, 0, :])
            return
    p.barrier()
    if stop_after == "p1b":
        xob = x_out.bitcast(BF16)
        for c in range(NCH):
            p.dma("act", xob[c * 128:(c + 1) * 128, 0:NT], ab[:, c, :])
        return

    WoT = w["l0_WoT"]
    for i in range(NCH):
        ld = p.dma("sp", W32[:], WoT[i * 128:(i + 1) * 128, :])
        p.wait("dve", "q_sp", ld)
        copy(p, "dve", wfull[:, i, :], W32[:])
        dvw = p.step("dve")
        p.wait("sp", "m_dve", dvw)
    wready = p.cnt["m_dve"]
    p.wait_eng("pe", "dve", wready)
    for tti in range(NTT):
        tok = slice(tti * TTILE, (tti + 1) * TTILE)
        for og in range(2):
            pset = ps[og * 4:(og + 1) * 4]
            p.wait("sp", "q_act", p.cnt["q_act"])
            p.wait("sp", "m_dve", p.cnt["m_dve"])
            lx = {}
            for o in range(4):
                oc = og * 4 + o
                lx[oc] = p.dma("sp", F1[:, o // 2, (o % 2) * 512:(o % 2 + 1) * 512],
                               x_in[oc * 128:(oc + 1) * 128, tok])
            p.wait_eng("pe", "act", p.cnt["m_act"])
            for o in range(4):
                oc = og * 4 + o
                for i in range(NCH):
                    mm(p, pset[o][:], wfull[:, i, oc * 128:(oc + 1) * 128],
                       ab[:, i, tok], start=(i == 0), stop=(i == NCH - 1))
            pec = p.step("pe")
            p.wait_eng("act", "pe", pec)
            for o in range(4):
                oc = og * 4 + o
                dst = SCR[:, (o % 2) * 512:(o % 2 + 1) * 512]
                p.wait_eng("act", "dve", p.cnt["m_dve"] if o >= 2 else 0)
                act(p, dst, pset[o][:], AF.Identity, bias=bcol[:, oc:oc + 1])
                avc = p.step("act")
                p.wait_eng("dve", "act", avc)
                p.wait("dve", "q_sp", lx[oc])
                xs = F1[:, o // 2, (o % 2) * 512:(o % 2 + 1) * 512]
                tt(p, "dve", xs, xs, dst, ALU.add)
                dvc = p.step("dve")
                p.wait("act", "m_dve", dvc)
                p.dma("act", xacc_d[oc * 128:(oc + 1) * 128, tok], xs)
    p.barrier()
    if stop_after == "p1c":
        for c in range(NCH):
            for h in range(2):
                ld = p.dma("sp", F1[:, h, :], xacc_d[c * 128:(c + 1) * 128, h * H2:(h + 1) * H2])
                p.wait("act", "q_sp", ld)
                p.dma("act", x_out[c * 128:(c + 1) * 128, h * H2:(h + 1) * H2], F1[:, h, :])
                p.wait("sp", "q_act", p.cnt["q_act"])
        return

    for c in range(NCH):
        p.wait_eng("act", "dve", p.cnt["m_dve"])
        act(p, F1[:].rearrange("p a b -> p (a b)"), xn[:, c, :], AF.Copy, scale=16.0, bias=CRND)
        avc = p.step("act")
        p.wait_eng("dve", "act", avc)
        f = F1[:].rearrange("p a b -> p (a b)")
        ts2(p, "dve", f, f, -CRND, -128.0, ALU.add, ALU.max)
        ts1(p, "dve", ab[:, c, :], f, 127.0, ALU.min)
        p.step("dve")
    w1T, w2T = w["l0_w1T"], w["l0_w2T"]
    def quant_weight_tile(src_ap, dst_ap, wscale):
        ld = p.dma("sp", W32[:], src_ap)
        p.wait("act", "q_sp", ld)
        p.wait_eng("act", "dve", p.cnt["m_dve"])
        act(p, SCR[:], W32[:], AF.Copy, scale=64.0, bias=CRND)
        avc = p.step("act")
        p.wait("sp", "m_act", avc)
        p.wait_eng("dve", "act", avc)
        p.wait("dve", "q_act", p.cnt["q_act"])
        ts2(p, "dve", SCR[:], SCR[:], -CRND, -128.0, ALU.add, ALU.max)
        ts2(p, "dve", QT[:], SCR[:], 127.0, wscale, ALU.min, ALU.mult)
        dvc = p.step("dve")
        p.wait("act", "m_dve", dvc)
        p.dma("act", dst_ap, QT[:].rearrange("p (h f) -> p h f", f=128))
    for i in range(NCH):
        for half in range(4):
            cols = slice(half * 1024, (half + 1) * 1024)
            quant_weight_tile(w1T[i * 128:(i + 1) * 128, cols],
                              w1q_d[half * 8:(half + 1) * 8, :, i, :].rearrange("h p f -> p h f"),
                              1.0 / 64)
    for hc32 in range(32):
        quant_weight_tile(w2T[hc32 * 128:(hc32 + 1) * 128, :],
                          w2q_d[:, :, hc32, :].rearrange("o p f -> p o f"), 1.0 / 1024)
    p.raw("dve", lambda E: E.memset(SM[:, 0:128], 0.0))
    dvz = p.step("dve")
    p.wait("sp", "m_dve", dvz)
    lb = p.dma("sp", SM[:, 0:32], w["l0_b1c"])
    lb = p.dma("sp", SM[:, 32:40], w["l0_b2c"])
    p.wait("dve", "q_sp", lb)
    ts2(p, "dve", SM[:, 128:256], SM[:, 0:128], 64.0, CRND, ALU.mult, ALU.add)
    ts2(p, "dve", SM[:, 128:256], SM[:, 128:256], -CRND, -128.0, ALU.add, ALU.max)
    ts2(p, "dve", SM[:, 128:256], SM[:, 128:256], 127.0, 1.0 / 64, ALU.min, ALU.mult)
    ts2(p, "dve", SM[:, 256:384], SM[:, 128:256], 16.0, 1.0, ALU.mult, ALU.mult)
    dvq = p.step("dve")
    p.wait_eng("dve", "dve" if False else "act", p.cnt["m_act"])
    copy(p, "dve", bcol[:, 24:56], SM[:, 256:288])
    copy(p, "dve", bcol[:, 56:64], SM[:, 160:168])
    p.step("dve")
    p.barrier()
    if stop_after == "p2a":
        xob = x_out.bitcast(BF16)
        for c in range(NCH):
            p.dma("act", xob[c * 128:(c + 1) * 128, 0:NT], ab[:, c, :])
        return

    TQ = 256
    hstrip = AH[:].rearrange("p (h f) -> p h f", f=TQ)
    w1s = W1S[:].rearrange("p (s a f) -> p s a f", s=2, f=128)
    act_hc, dve_hc, pool_hc, w2_set_act = {}, {}, {}, {}
    if stop_after == "p2w":
        xob = x_out.bitcast(BF16)
        for hh in range(4):
            p.dma("act", xob[0:128, hh * 1024:(hh + 1) * 1024],
                  w1q_d[hh].rearrange("p i f -> p (i f)"))
        p.dma("act", xob[128:256, 0:1024], w2q_d[0, :, 0:8, :].rearrange("p o f -> p (o f)"))
        return
    for tq in range(NT // TQ):
        tok = slice(tq * TQ, (tq + 1) * TQ)
        for hc in range(32):
            pa = ps[hc % 2]
            p.wait("sp", "m_pe", max(0, p.cnt["m_pe"] - 1))
            ld = p.dma("sp", w1s[:, hc % 2, :, :], w1q_d[hc])
            p.wait("pe", "q_sp", ld)
            p.wait_eng("pe", "act", act_hc.get(hc - 2 if hc >= 2 else 30, 0))
            for i in range(NCH):
                mm(p, pa[:, 0:TQ], w1s[:, hc % 2, i, :], ab[:, i, tok],
                   start=(i == 0), stop=(i == NCH - 1))
            pec = p.step("pe")
            p.wait_eng("act", "pe", pec)
            p.wait_eng("act", "dve", dve_hc.get(hc - 2 if hc >= 2 else 30, 0))
            sdst = SCR[:, (hc % 2) * 256:(hc % 2) * 256 + 256]
            act(p, sdst, pa[:, 0:TQ], AF.Identity, bias=bcol[:, 24 + hc:25 + hc])
            act_hc[hc] = p.step("act")
            p.wait_eng("dve", "act", act_hc[hc])
            p.wait_eng("dve", "pool", pool_hc.get(hc, 0))
            if stop_after == "p2h3":
                copy(p, "dve", hstrip[:, hc, :], sdst)
                dve_hc[hc] = p.step("dve")
                pool_hc[hc] = p.cnt["m_pool"]
            elif stop_after == "p2h2":
                ts2(p, "dve", hstrip[:, hc, :], sdst, CRND, -CRND, ALU.add, ALU.add)
                ts2(p, "dve", hstrip[:, hc, :], hstrip[:, hc, :], 0.0, 127.0, ALU.max, ALU.min)
                dve_hc[hc] = p.step("dve")
                pool_hc[hc] = p.cnt["m_pool"]
            else:
                ts2(p, "dve", hstrip[:, hc, :], sdst, CRND, -CRND, ALU.add, ALU.add)
                dve_hc[hc] = p.step("dve")
                p.wait_eng("pool", "dve", dve_hc[hc])
                ts2(p, "pool", hstrip[:, hc, :], hstrip[:, hc, :], 0.0, 127.0, ALU.max, ALU.min)
                pool_hc[hc] = p.step("pool")
        poolc = p.cnt["m_pool"]
        if stop_after in ("p2h", "p2h2", "p2h3"):
            p.barrier()
            xob = x_out.bitcast(BF16)
            p.dma("act", xob[0:128, 0:8192], AH[:])
            p.dma("act", x_out[128:256, 0:64], bcol[:])
            return
        for og in range(4):
            pset = [ps[2 + (og % 3) * 2], ps[3 + (og % 3) * 2]]
            p.wait_eng("pe", "pool", poolc)
            p.wait_eng("pe", "dve", w2_set_act.get(og % 3, 0))
            for o in range(2):
                oc = og * 2 + o
                for grp in range(4):
                    p.wait("sp", "m_pe", max(0, p.cnt["m_pe"] - 1))
                    ld = p.dma("sp", w1s[:, grp % 2, :, :],
                               w2q_d[oc, :, grp * 8:(grp + 1) * 8, :])
                    p.wait("pe", "q_sp", ld)
                    for hci in range(8):
                        hc = grp * 8 + hci
                        mm(p, pset[o][:, 0:TQ], w1s[:, grp % 2, hci, :], hstrip[:, hc, :],
                           start=(hc == 0), stop=(hc == 31))
                    p.step("pe")
            pec = p.cnt["m_pe"]
            p.wait_eng("act", "pe", pec)
            for o in range(2):
                oc = og * 2 + o
                lx = p.dma("act", F2[:, o, 0:TQ], xacc_d[oc * 128:(oc + 1) * 128, tok])
                dst = SCR[:, 512 + o * 256:512 + o * 256 + 256]
                p.wait_eng("act", "dve", p.cnt["m_dve"])
                act(p, dst, pset[o][:, 0:TQ], AF.Identity, bias=bcol[:, 56 + oc:57 + oc])
                avc = p.step("act")
                p.wait_eng("dve", "act", avc)
                p.wait("dve", "q_act", lx)
                tt(p, "dve", F2[:, o, 0:TQ], F2[:, o, 0:TQ], dst, ALU.add)
                dvc = p.step("dve")
                p.wait("act", "m_dve", dvc)
                p.dma("act", x_out[oc * 128:(oc + 1) * 128, tok], F2[:, o, 0:TQ])
            w2_set_act[og % 3] = p.cnt["m_dve"]
    p.barrier()


def l0_host_inputs(inp):
    d = {}
    W6 = np.stack([inp["L0_attn_W_r"], inp["L0_attn_W_w"], inp["L0_attn_W_k"],
                   inp["L0_attn_W_v"], inp["L0_attn_W_a"], inp["L0_attn_W_g"]])
    d["l0_W6T"] = np.ascontiguousarray(np.transpose(W6, (0, 2, 1)))
    d["l0_WoT"] = np.ascontiguousarray(inp["L0_attn_W_out"].T)
    mix = np.stack([inp["L0_attn_x_r"], inp["L0_attn_x_w"], inp["L0_attn_x_k"],
                    inp["L0_attn_x_v"], inp["L0_attn_x_a"], inp["L0_attn_x_g"]])
    mix = mix.reshape(6, D)
    d["l0_mix"] = np.ascontiguousarray(mix.reshape(6, NCH, 128).transpose(2, 1, 0))
    d["l0_bout"] = col8(inp["L0_attn_b_out"])
    d["l0_gnw"] = col8(inp["L0_attn_gn_w"])
    d["l0_gnb"] = col8(inp["L0_attn_gn_b"])
    d["l0_w1T"] = np.ascontiguousarray(inp["L0_ffn_w1"].T)
    d["l0_w2T"] = np.ascontiguousarray(inp["L0_ffn_w2"].T)
    d["l0_b1c"] = col8(inp["L0_ffn_b1"])
    d["l0_b2c"] = col8(inp["L0_ffn_b2"])
    return d


def col8(v):
    v = np.asarray(v, np.float32)
    n = v.size // 128
    return np.ascontiguousarray(v.reshape(n, 128).T)


def consts_host():
    bd = np.zeros((128, 128), np.float32)
    for g in range(2):
        bd[g * 64:(g + 1) * 64, g * 64:(g + 1) * 64] = 1.0
    import ml_dtypes
    cv = np.zeros((128, 8), np.float32)
    cv[:, 0] = LN_EPS
    cv[:, 1] = GN_EPS
    cv[:, 2] = CRND
    tri = (np.arange(128)[:, None] <= np.arange(128)[None, :]).astype(np.float32)
    return {"bd64": bd.astype(ml_dtypes.bfloat16),
            "ones128": np.ones((128, 128), ml_dtypes.bfloat16),
            "triU": tri.astype(ml_dtypes.bfloat16),
            "ident": np.eye(128, dtype=np.float32).astype(ml_dtypes.bfloat16),
            "identf": np.eye(128, dtype=np.float32),
            "cvec": cv}



def build_launch_l0(stop_after=None):
    p = Prog("l0")
    x_in = p.inp("x_fm", [D, NT], F32)
    w = {}
    w["l0_W6T"] = p.inp("l0_W6T", [6, D, D], F32)
    w["l0_WoT"] = p.inp("l0_WoT", [D, D], F32)
    w["l0_mix"] = p.inp("l0_mix", [128, NCH, 6], F32)
    w["l0_bout"] = p.inp("l0_bout", [128, 8], F32)
    w["l0_gnw"] = p.inp("l0_gnw", [128, 8], F32)
    w["l0_gnb"] = p.inp("l0_gnb", [128, 8], F32)
    w["l0_w1T"] = p.inp("l0_w1T", [D, 4096], F32)
    w["l0_w2T"] = p.inp("l0_w2T", [4096, D], F32)
    w["l0_b1c"] = p.inp("l0_b1c", [128, 32], F32)
    w["l0_b2c"] = p.inp("l0_b2c", [128, 8], F32)
    w["bd64"] = p.inp("bd64", [128, 128], BF16)
    w["cvec"] = p.inp("cvec", [128, 8], F32)
    w["ones128"] = p.inp("ones128", [128, 128], BF16)
    x_out = p.outp("x1_fm", [D, NT], F32)
    emit_l0(p, x_in, x_out, w, stop_after=stop_after)
    p.finalize()
    return p


_PROG_CACHE = {}


def _prog(name, builder):
    if name not in _PROG_CACHE:
        _PROG_CACHE[name] = builder()
    return _PROG_CACHE[name]


def kernel(**inputs):
    import ml_dtypes
    consts = consts_host()

    def launch(p, in_maps):
        fm = [{k: v for k, v in m.items() if k in p.inputs} for m in in_maps]
        return run_bass_kernel_spmd(p.nc, fm, core_ids=list(range(NCORE)))

    p0 = _prog("l0", build_launch_l0)
    w0 = l0_host_inputs(inputs)
    w0.update(consts)
    x = np.asarray(inputs["L0_attn_x"], np.float32)
    ims = []
    for c in range(NCORE):
        m = {"x_fm": np.ascontiguousarray(x[c * 2:(c + 1) * 2].reshape(NT, D).T)}
        m.update(w0)
        ims.append(m)
    r0 = launch(p0, ims)
    x1f = [np.asarray(r0.results[c]["x1_fm"]) for c in range(NCORE)]

    p1 = _prog("l1", build_launch_l1)
    ims = []
    for c in range(NCORE):
        w1 = l1_host_inputs(inputs, c)
        w1.update(consts)
        m = {"x1_fm": x1f[c]}
        m.update(w1)
        ims.append(m)
    r1 = launch(p1, ims)
    x2f = [np.asarray(r1.results[c]["x2_fm"]) for c in range(NCORE)]

    w2 = l2_host_inputs(inputs)
    x2full = np.concatenate(x2f, axis=1)
    p2a = _prog("l2a", build_launch_l2a)
    ims = []
    for c in range(NCORE):
        rot = np.ascontiguousarray(np.concatenate(
            [x2full[:, c * NT:], x2full[:, :c * NT]], axis=1))
        m = {"x2f": rot}
        m.update(w2)
        m.update(consts)
        ims.append(m)
    r2a = launch(p2a, ims)
    statAg = np.ascontiguousarray(np.stack(
        [np.asarray(r2a.results[c]["statA"], np.float32).reshape(48) for c in range(NCORE)], axis=1))
    p2b = _prog("l2b", build_launch_l2b)
    ims = []
    for c in range(NCORE):
        m = {"qkvo": np.asarray(r2a.results[c]["qkvo"]), "statAg": statAg}
        m.update(consts)
        ims.append(m)
    r2b = launch(p2b, ims)
    statBg = np.ascontiguousarray(np.stack(
        [np.asarray(r2b.results[c]["statB"], np.float32).reshape(2) for c in range(NCORE)], axis=1))
    p2c = _prog("l2c", build_launch_l2c)
    ims = []
    for c in range(NCORE):
        m = {"x2o": x2f[c], "xn2o": np.asarray(r2a.results[c]["xn2o"]),
             "aoo": np.asarray(r2b.results[c]["aoo"]), "statBg": statBg}
        m.update(w2)
        m.update(consts)
        ims.append(m)
    r2c = launch(p2c, ims)
    x3f = [np.asarray(r2c.results[c]["x3o"]) for c in range(NCORE)]

    w3 = l3_host_inputs(inputs)
    x3full = np.concatenate(x3f, axis=1)
    p3 = _prog("l3", build_launch_l3)
    ims = []
    for c in range(NCORE):
        rot = np.ascontiguousarray(np.concatenate(
            [x3full[:, c * NT:], x3full[:, :c * NT]], axis=1))
        m = {"x3f": rot}
        m.update(w3)
        m.update(consts)
        ims.append(m)
    r3 = launch(p3, ims)

    out = np.empty((16, 2048, 1024), np.float32)
    for c in range(NCORE):
        out[c * 2:(c + 1) * 2] = np.asarray(
            r3.results[c]["x4o"], np.float32).T.reshape(2, 2048, 1024)
    return out



MPAD = 512
NCK = 32


def rep_mid(ap, n):
    return ap.unsqueeze(1).to_broadcast([ap.shape[0], n, ap.shape[1]])


def rep_in(ap, n):
    return ap.unsqueeze(2).to_broadcast([ap.shape[0], ap.shape[1], n])


def emit_l1(p, x_in, x_out, w, stop_after=None):
    nc = p.nc
    xn = p.sbuf("l1_xn", [128, NCH, NT], BF16)
    Yf = p.sbuf("l1_yf", [128, NCH, NT], BF16)
    U32 = p.sbuf("l1_u32", [128, 8192], F32)
    U16 = p.sbuf("l1_u16", [128, 16384], BF16)
    SMALL = p.sbuf("l1_small", [128, 4096], F32)
    F1 = None
    F2 = None
    SCR = p.sbuf("l1_scr", [128, 1024], F32)
    SM = p.sbuf("l1_sm", [128, 1024], F32)
    QT = p.sbuf("l1_qt", [128, 1024], BF16)
    cvec = p.sbuf("l1_cvec", [128, 8], F32)
    ones128 = p.sbuf("l1_ones", [128, 128], BF16)
    tri128 = p.sbuf("l1_tri", [128, 128], BF16)
    ident = p.sbuf("l1_id", [128, 128], BF16)
    identf = p.sbuf("l1_idf", [128, 128], F32)
    carry = p.sbuf("l1_carry", [128, 64], F32)
    carb = p.sbuf("l1_carb", [128, 64], BF16)
    gbias_bc = p.sbuf("l1_gb", [128, 32], F32)
    ebias_bc = p.sbuf("l1_eb", [128, 32], F32)
    bdt_bc = p.sbuf("l1_bdtbc", [128, 16], F32)
    negA_bc = p.sbuf("l1_negA", [128, 16], F32)
    idxs = p.sbuf("l1_idxs", [128, 16], I32)
    poscol = p.sbuf("l1_poss", [128, 16], F32)
    ps = [p.psum(f"l1_ps{i}", [128, 512], F32) for i in range(8)]
    pst = ps[6][:].bitcast(BF16)[:, 0:128]
    F1 = U32[:, 0:4096].rearrange("p (a b) -> p a b", b=H2)
    F2 = U32[:, 4096:8192].rearrange("p (a b) -> p a b", b=H2)
    ar = {"F1": F1, "F2": F2, "SM": SM, "SCR": SCR, "QT": QT, "ps": ps, "cvec": cvec}

    dtp = SMALL[:, 0:512]
    Atm = SMALL[:, 512:1024]
    cums = SMALL[:, 1024:1536]
    wtm = SMALL[:, 1536:2048]
    sds = SMALL[:, 2048:2560]
    dtw = SMALL[:, 2560:3072]
    totbc = SMALL[:, 3072:3584]
    Pbc = SMALL[:, 3584:4096]

    xn_tm_d = p.dram("l1_xntm", [NT, D], BF16)
    X_d = p.dram("l1_X", [NCK, 128, D], BF16)
    Xw_d = p.dram("l1_Xw", [NCK, 128, D], BF16)
    Bw_d = p.dram("l1_Bw", [NCK, 128, D], BF16)
    Bfm_d = p.dram("l1_Bfm", [NCH, 128, NT], BF16)
    Cp_d = p.dram("l1_Cp", [NCH, 128, NT], BF16)
    Cf_d = p.dram("l1_Cf", [NCH, 128, NT], BF16)
    zs_d = p.dram("l1_zs", [NCH, 128, NT], BF16)
    sh_d = p.dram("l1_sh", [4, 128, NT], BF16)
    sds2_d = p.dram("l1_sds2", [4, 128, 128], F32)
    cums2_d = p.dram("l1_cums2", [4, 128, 128], F32)

    def trow(dram3, h, tile):
        r = dram3[tile // 8].rearrange("(t h) p -> h t p", h=16)[h, tile % 8]
        return r.partition_broadcast(128)

    def trow4(dram3, h, tti, npart):
        k = (tti * 4) // 8
        t0 = (tti * 4) % 8
        r = dram3[k].rearrange("(t h) p -> h t p", h=16)[h, t0:t0 + 4]
        return r.partition_broadcast(npart)
    tot_d = p.dram("l1_tot", [1, 512], F32)
    nega_d = p.dram("l1_negad", [1, 16], F32)
    mask_d = p.dram("l1_maskd", [4, 32, 128], F32)
    gat_d = p.dram("l1_gatd", [4, 8, 128], F32)
    corrtm_d = p.dram("l1_corrtm", [16, 128, D], BF16)
    xacc_d = p.dram("l1_xacc", [D, NT], F32)

    p.dma("sp", cvec[:], w["cvec"])
    p.dma("sp", ones128[:], w["ones128"])
    p.dma("sp", tri128[:], w["triU"])
    p.dma("sp", ident[:], w["ident"])
    p.dma("sp", identf[:], w["identf"])
    p.dma("sp", gbias_bc[:], w["l1_gbias"].to_broadcast([128, 32]))
    p.dma("sp", ebias_bc[:], w["l1_ebias"].to_broadcast([128, 32]))
    p.dma("sp", bdt_bc[:], w["l1_bdt"].to_broadcast([128, 16]))
    p.dma("sp", idxs[:], w["l1_moeidx"])
    cl = p.dma("sp", poscol[:], w["l1_moepos"])
    for e in ENGINES:
        p.wait(e, "q_sp", cl)

    emit_ln(p, x_in, xn, ar, ones128, "l1")

    for c in range(NCH):
        for t4 in range(NT // 128):
            sl = ((c * 32 + t4) % 2) * 128
            p.wait("sp", "q_act", max(0, p.cnt["q_act"] - 16))
            ld = p.dma("sp", QT[:, sl:sl + 128], xn[:, c, t4 * 128:(t4 + 1) * 128],
                       transpose=True)
            p.wait("act", "q_sp", ld)
            p.dma("act", xn_tm_d[t4 * 128:(t4 + 1) * 128, c * 128:(c + 1) * 128],
                  QT[:, sl:sl + 128])
    p.barrier()

    la = p.dma("sp", SM[:, 0:16][0:1, :], w["l1_Alog"])
    p.wait("act", "q_sp", la)
    act(p, SM[0:1, 16:32], SM[0:1, 0:16], AF.Exp)
    p.step("act")
    p.wait_eng("dve", "act", p.cnt["m_act"])
    ts1(p, "dve", SM[0:1, 32:48], SM[0:1, 16:32], -1.0, ALU.mult)
    dv = p.step("dve")
    p.wait("act", "m_dve", dv)
    st = p.dma("act", nega_d, SM[0:1, 32:48])
    p.wait("sp", "q_act", st)
    ld = p.dma("sp", negA_bc[:], nega_d.to_broadcast([128, 16]))
    ldw = p.dma("sp", SM[:, 128:256].rearrange("p (i n) -> p i n", n=16), w["l1_WdtT"].rearrange("(i p) n -> p i n", p=128))
    p.wait("dve", "q_sp", ldw)
    copy(p, "dve", QT[:, 0:128], SM[:, 128:256])
    dv = p.step("dve")
    wdt16 = QT[:, 0:128].rearrange("p (i n) -> p i n", n=16)
    p.wait_eng("pe", "dve", dv)
    for t4 in range(NCK):
        pa = ps[t4 % 2]
        p.wait_eng("pe", "act", p.cnt["m_act"] if t4 >= 2 else 0)
        for i in range(NCH):
            mm(p, pa[:, 0:16], xn[:, i, t4 * 128:(t4 + 1) * 128], wdt16[:, i, :],
               start=(i == 0), stop=(i == NCH - 1))
        pec = p.step("pe")
        p.wait_eng("act", "pe", pec)
        act(p, dtp[:, t4 * 16:(t4 + 1) * 16], pa[:, 0:16], AF.Copy, scale=1.0, bias=0.0)
        p.step("act")
    p.barrier()
    dtp3 = dtp.rearrange("p (a b) -> p a b", b=16)
    tt(p, "dve", dtp3, dtp3, rep_mid(bdt_bc[:], NCK), ALU.add)
    dv = p.step("dve")
    p.wait_eng("act", "dve", dv)
    act(p, U32[:, 0:512], dtp, AF.Abs)
    p.raw("act", lambda E: E.drain())
    act(p, U32[:, 512:1024], U32[:, 0:512], AF.Exp, scale=-1.0)
    p.raw("act", lambda E: E.drain())
    act(p, U32[:, 1024:1536], U32[:, 512:1024], AF.Ln, bias=1.0)
    p.raw("act", lambda E: E.drain())
    act(p, U32[:, 1536:2048], dtp, AF.Relu)
    av = p.step("act")
    p.wait_eng("dve", "act", av)
    tt(p, "dve", dtp, U32[:, 1024:1536], U32[:, 1536:2048], ALU.add)
    p.wait("dve", "q_sp", ld)
    tt(p, "dve", Atm.rearrange("p (a b) -> p a b", b=16), dtp3, rep_mid(negA_bc[:], NCK), ALU.mult)
    p.raw("dve", lambda E: E.drain())
    copy(p, "dve", QT[:, 128:640], Atm)
    dv = p.step("dve")
    Abf = QT[:, 128:640]
    p.wait_eng("pe", "dve", dv)
    for t4 in range(NCK):
        pa = ps[t4 % 2]
        p.wait_eng("pe", "act", p.cnt["m_act"] if t4 >= 2 else 0)
        mm(p, pa[:, 0:16], tri128[:], Abf[:, t4 * 16:(t4 + 1) * 16], start=True, stop=True)
        pec = p.step("pe")
        p.wait_eng("act", "pe", pec)
        act(p, cums[:, t4 * 16:(t4 + 1) * 16], pa[:, 0:16], AF.Copy, scale=1.0, bias=0.0)
        p.step("act")
    av = p.step("act")
    p.wait("act", "m_act", av)
    st = p.dma("act", tot_d, cums[127:128, :])
    p.wait("sp", "q_act", st)
    ld = p.dma("sp", totbc, tot_d.to_broadcast([128, 512]))
    p.wait("dve", "q_sp", ld)
    tt(p, "dve", wtm, totbc, cums, ALU.subtract)
    dv = p.step("dve")
    p.wait_eng("act", "dve", dv)
    act(p, wtm, wtm, AF.Exp)
    p.raw("act", lambda E: E.drain())
    act(p, sds, cums, AF.Exp)
    p.raw("act", lambda E: E.drain())
    act(p, Pbc, totbc, AF.Exp)
    av = p.step("act")
    p.wait_eng("dve", "act", av)
    tt(p, "dve", dtw, dtp, wtm, ALU.mult)
    dv = p.step("dve")
    p.barrier()
    pstf = ps[6][:, 0:128]
    kk2 = 0
    for src_t, dst3 in ((sds, sds2_d), (cums, cums2_d)):
        for k in range(4):
            par2 = kk2 % 2
            kk2 += 1
            p.wait_eng("pe", "act", p.cnt["m_act"])
            p.raw("pe", lambda E, s=src_t, k=k: E.transpose(
                out=pstf, in_=s[:, k * 128:(k + 1) * 128], identity=identf[:]))
            pec = p.step("pe")
            p.wait_eng("act", "pe", pec)
            p.wait("act", "q_act", max(0, p.cnt["q_act"] - 16))
            act(p, SCR[:, par2 * 128:(par2 + 1) * 128], pstf, AF.Copy, scale=1.0, bias=0.0)
            p.step("act")
            p.dma("act", dst3[k], SCR[:, par2 * 128:(par2 + 1) * 128])
    p.barrier()
    if stop_after == "q1b":
        p.dma("act", x_out[0:128, 0:512], dtp)
        p.dma("act", x_out[0:128, 512:1024], cums)
        p.dma("act", x_out[0:128, 1024:1536], wtm)
        p.dma("act", x_out[0:128, 1536:2048], sds)
        return

    wblk = U16[:, 0:8192].rearrange("p (i n) -> p i n", n=1024)

    def load_wblk(src_ap):
        dv0 = p.cnt["m_dve"]
        for i in range(NCH):
            p.wait("sp", "m_dve", p.cnt["m_dve"])
            ldx = p.dma("sp", SCR[:], src_ap[i * 128:(i + 1) * 128, :])
            p.wait("dve", "q_sp", ldx)
            p.wait_eng("dve", "pe", p.cnt["m_pe"])
            copy(p, "dve", wblk[:, i, :], SCR[:])
            p.step("dve")
        return p.cnt["m_dve"]

    wr = load_wblk(w["l1_WinT"])
    p.wait_eng("pe", "dve", wr)
    for t4 in range(NCK):
        for nb in range(2):
            pa = ps[(t4 * 2 + nb) % 2]
            p.wait_eng("pe", "dve", p.cnt["m_dve"])
            for i in range(NCH):
                mm(p, pa[:], xn[:, i, t4 * 128:(t4 + 1) * 128], wblk[:, i, nb * 512:(nb + 1) * 512],
                   start=(i == 0), stop=(i == NCH - 1))
            pec = p.step("pe")
            p.wait_eng("act", "pe", pec)
            p.wait_eng("act", "dve", p.cnt["m_dve"])
            act(p, U32[:, 6656 + nb * 512: 6656 + (nb + 1) * 512], pa[:], AF.Copy, scale=1.0, bias=0.0)
            p.step("act")
        avc = p.cnt["m_act"]
        p.wait_eng("dve", "act", avc)
        p.wait("dve", "q_act", p.cnt["q_act"])
        xin_t = U32[:, 6656:7680].rearrange("p (h q) -> p h q", q=64)
        dcol = dtp[:, t4 * 16:(t4 + 1) * 16]
        wcol = dtw[:, t4 * 16:(t4 + 1) * 16]
        tt(p, "dve", QT[:, 0:1024].rearrange("p (h q) -> p h q", q=64), xin_t, rep_in(dcol, 64), ALU.mult)
        p.raw("dve", lambda E: E.drain())
        tt(p, "dve", U16[:, 8192:9216].rearrange("p (h q) -> p h q", q=64), xin_t, rep_in(wcol, 64), ALU.mult)
        dvc = p.step("dve")
        p.wait("act", "m_dve", dvc)
        p.dma("act", X_d[t4], QT[:, 0:1024])
        p.dma("act", Xw_d[t4], U16[:, 8192:9216])
    p.barrier()
    wr = load_wblk(w["l1_WinTz"])
    p.wait_eng("pe", "dve", wr)
    for oc in range(NCH):
        for tti in range(NTT):
            pa = ps[(oc * NTT + tti) % 2]
            p.wait_eng("pe", "act", p.cnt["m_act"] if (oc + tti) > 0 else 0)
            for i in range(NCH):
                mm(p, pa[:], wblk[:, i, oc * 128:(oc + 1) * 128] if False else wblk[:, i, oc * 128:(oc + 1) * 128],
                   xn[:, i, tti * TTILE:(tti + 1) * TTILE], start=(i == 0), stop=(i == NCH - 1))
            pec = p.step("pe")
            p.wait_eng("act", "pe", pec)
            p.wait("act", "q_act", max(0, p.cnt["q_act"] - 16))
            act(p, QT[:, (tti % 2) * 512:(tti % 2 + 1) * 512], pa[:], AF.Silu)
            p.step("act")
            p.dma("act", zs_d[oc, :, tti * TTILE:(tti + 1) * TTILE], QT[:, (tti % 2) * 512:(tti % 2 + 1) * 512])
    p.barrier()
    wr = load_wblk(w["l1_WBT"])
    p.wait_eng("pe", "dve", wr)
    for oc in range(NCH):
        for tti in range(NTT):
            pa = ps[(oc * NTT + tti) % 2]
            p.wait_eng("pe", "act", p.cnt["m_act"] if (oc + tti) > 0 else 0)
            for i in range(NCH):
                mm(p, pa[:], wblk[:, i, oc * 128:(oc + 1) * 128],
                   xn[:, i, tti * TTILE:(tti + 1) * TTILE], start=(i == 0), stop=(i == NCH - 1))
            pec = p.step("pe")
            p.wait_eng("act", "pe", pec)
            p.wait("act", "q_act", max(0, p.cnt["q_act"] - 16))
            act(p, QT[:, (tti % 2) * 512:(tti % 2 + 1) * 512], pa[:], AF.Copy, scale=1.0, bias=0.0)
            p.step("act")
            p.dma("act", Bfm_d[oc, :, tti * TTILE:(tti + 1) * TTILE], QT[:, (tti % 2) * 512:(tti % 2 + 1) * 512])
    p.barrier()
    for t4 in range(NCK):
        for nb in range(2):
            pa = ps[(t4 * 2 + nb) % 2]
            p.wait_eng("pe", "dve", p.cnt["m_dve"])
            for i in range(NCH):
                mm(p, pa[:], xn[:, i, t4 * 128:(t4 + 1) * 128], wblk[:, i, nb * 512:(nb + 1) * 512],
                   start=(i == 0), stop=(i == NCH - 1))
            pec = p.step("pe")
            p.wait_eng("act", "pe", pec)
            p.wait_eng("act", "dve", p.cnt["m_dve"])
            act(p, U32[:, 6656 + nb * 512: 6656 + (nb + 1) * 512], pa[:], AF.Copy, scale=1.0, bias=0.0)
            p.step("act")
        avc = p.cnt["m_act"]
        p.wait_eng("dve", "act", avc)
        p.wait("dve", "q_act", p.cnt["q_act"])
        bin_t = U32[:, 6656:7680].rearrange("p (h q) -> p h q", q=64)
        wcol = wtm[:, t4 * 16:(t4 + 1) * 16]
        tt(p, "dve", U16[:, 8192:9216].rearrange("p (h q) -> p h q", q=64), bin_t, rep_in(wcol, 64), ALU.mult)
        dvc = p.step("dve")
        p.wait("act", "m_dve", dvc)
        p.dma("act", Bw_d[t4], U16[:, 8192:9216])
    p.barrier()
    wr = load_wblk(w["l1_WCT"])
    p.wait_eng("pe", "dve", wr)
    for oc in range(NCH):
        for tti in range(NTT):
            pa = ps[(oc * NTT + tti) % 2]
            p.wait_eng("pe", "dve", p.cnt["m_dve"])
            for i in range(NCH):
                mm(p, pa[:], wblk[:, i, oc * 128:(oc + 1) * 128],
                   xn[:, i, tti * TTILE:(tti + 1) * TTILE], start=(i == 0), stop=(i == NCH - 1))
            pec = p.step("pe")
            p.wait_eng("act", "pe", pec)
            p.wait_eng("act", "dve", p.cnt["m_dve"])
            act(p, U32[:, 6656:7168], pa[:], AF.Copy, scale=1.0, bias=0.0)
            avc = p.step("act")
            lo = p.dma("sp", U32[0:64, 7168:7680].rearrange("p (a b) -> p a b", b=128),
                       trow4(sds2_d, 2 * oc, tti, 64))
            lo = p.dma("sp", U32[64:128, 7168:7680].rearrange("p (a b) -> p a b", b=128),
                       trow4(sds2_d, 2 * oc + 1, tti, 64))
            p.wait("dve", "q_sp", lo)
            p.wait_eng("dve", "act", avc)
            p.wait("dve", "q_act", p.cnt["q_act"])
            copy(p, "dve", U16[:, 8704:9216], U32[:, 6656:7168])
            p.raw("dve", lambda E: E.drain())
            tt(p, "dve", U16[:, 8192:8704], U32[:, 6656:7168], U32[:, 7168:7680], ALU.mult)
            dvc = p.step("dve")
            p.wait("act", "m_dve", dvc)
            p.wait("sp", "m_dve", dvc)
            p.dma("act", Cf_d[oc, :, tti * TTILE:(tti + 1) * TTILE], U16[:, 8704:9216])
            p.dma("act", Cp_d[oc, :, tti * TTILE:(tti + 1) * TTILE], U16[:, 8192:8704])
    p.barrier()
    if stop_after == "q1c":
        return
    if stop_after in ("q1cd", "q1ce", "q1cf"):
        xob = x_out.bitcast(BF16)
        pairs = {"q1cd": (Bfm_d, Cf_d), "q1ce": (X_d, Xw_d), "q1cf": (Bw_d, Cp_d)}
        t0_, t1_ = pairs[stop_after]
        for half, tn in ((0, t0_), (1, t1_)):
            if tn.shape[0] == NCH:
                for c in range(NCH):
                    ld = p.dma("sp", U16[:, 0:4096], tn[c])
                    p.wait("act", "q_sp", ld)
                    st = p.dma("act", xob[c * 128:(c + 1) * 128, half * 4096:(half + 1) * 4096], U16[:, 0:4096])
                    p.wait("sp", "q_act", st)
            else:
                for tile in range(NCK):
                    ld = p.dma("sp", U16[:, 0:1024], tn[tile])
                    p.wait("act", "q_sp", ld)
                    st = p.dma("act", xob[(tile % 8) * 128:(tile % 8 + 1) * 128,
                                          half * 4096 + (tile // 8) * 1024:half * 4096 + (tile // 8 + 1) * 1024],
                               U16[:, 0:1024])
                    p.wait("sp", "q_act", st)
        return

    def gb(par, k, wdt):
        off = par * 1024 + k
        return U16[:, 9216 + off: 9216 + off + wdt]
    def ga(par, k, wdt):
        off = par * 1024 + k
        return U32[:, 0 + off: 0 + off + wdt]

    ld_done = {}
    dv_done = {}
    pe_done2 = {}
    ac_done = {}
    for b in range(B_LOC):
        for h in range(16):
            p.raw("dve", lambda E: E.memset(carry[0:64, :], 0.0))
            p.step("dve")
            for ci in range(16):
                tile = b * 16 + ci
                par = (h * 16 + ci) % 2
                oc = h // 2
                r0 = (h % 2) * 64
                gkey = (b, h, ci)
                pkey = dv_done.get(("grp", 1 - par), 0)
                p.wait("sp", "m_pe", pe_done2.get(("grp", par), 0))
                p.wait("sp", "m_dve", dv_done.get(("grp", par), 0))
                p.dma("sp", gb(par, 128, 128)[0:64, :], Bfm_d[oc, r0:r0 + 64, tile * 128:(tile + 1) * 128])
                p.dma("sp", gb(par, 256, 128)[0:64, :], Cp_d[oc, r0:r0 + 64, tile * 128:(tile + 1) * 128])
                p.dma("sp", gb(par, 640, 128)[0:64, :], Cf_d[oc, r0:r0 + 64, tile * 128:(tile + 1) * 128])
                p.dma("sp", gb(par, 384, 64), X_d[tile, :, h * 64:(h + 1) * 64])
                p.dma("sp", gb(par, 448, 64), Xw_d[tile, :, h * 64:(h + 1) * 64])
                p.dma("sp", gb(par, 512, 64), Bw_d[tile, :, h * 64:(h + 1) * 64])
                lda = p.dma("sp", ga(par, 0, 128), trow(cums2_d, h, tile))
                ld_done[gkey] = lda
                p.wait("pe", "q_sp", lda)
                p.wait_eng("pe", "act", ac_done.get(("G", par), 0))
                mm(p, ps[4 + par][:, 0:128], gb(par, 128, 128)[0:64, :], gb(par, 640, 128)[0:64, :],
                   start=True, stop=True)
                peg = p.step("pe")
                p.wait_eng("pe", "act", ac_done.get(("S", par), 0))
                mm(p, ps[6 + par][0:64, 0:64], gb(par, 512, 64), gb(par, 448, 64), start=True, stop=True)
                pes = p.step("pe")
                p.wait_eng("act", "pe", pes)
                p.wait_eng("act", "dve", dv_done.get(("Gsb", par), 0))
                act(p, ga(par, 128, 128), ps[4 + par][:, 0:128], AF.Copy, scale=1.0, bias=0.0)
                ac_done[("G", par)] = p.step("act")
                p.wait_eng("act", "dve", dv_done.get(("Sc", par), 0))
                act(p, ga(par, 256, 64)[0:64, :], ps[6 + par][0:64, 0:64], AF.Copy, scale=1.0, bias=0.0)
                ac_done[("S", par)] = p.step("act")
                p.wait("dve", "q_sp", lda)
                p.wait_eng("dve", "act", ac_done[("G", par)])
                ccol = cums[:, tile * 16 + h: tile * 16 + h + 1]
                tt(p, "dve", ga(par, 512, 128), ga(par, 0, 128), ccol.to_broadcast([128, 128]), ALU.subtract)
                p.raw("dve", lambda E: E.drain())
                ts1(p, "dve", ga(par, 512, 128), ga(par, 512, 128), 0.0, ALU.min)
                dvt = p.step("dve")
                p.wait_eng("act", "dve", dvt)
                act(p, ga(par, 640, 128), ga(par, 512, 128), AF.Exp)
                ace = p.step("act")
                p.wait_eng("dve", "act", ace)
                tt(p, "dve", ga(par, 640, 128), ga(par, 640, 128), tri128[:], ALU.mult)
                p.raw("dve", lambda E: E.drain())
                tt(p, "dve", gb(par, 0, 128), ga(par, 640, 128), ga(par, 128, 128), ALU.mult)
                dv_mb = p.step("dve")
                dv_done[("Gsb", par)] = dv_mb
                p.wait_eng("dve", "pe", pe_done2.get(("Y", 0), 0))
                copy(p, "dve", carb[0:64, :], carry[0:64, :])
                dv_cb = p.step("dve")
                p.wait_eng("pe", "dve", dv_mb)
                p.wait_eng("pe", "dve", dv_cb)
                p.wait_eng("pe", "act", ac_done.get(("Y",), 0))
                mm(p, ps[2 + par][:, 0:128][0:64, :] if False else ps[2 + par][0:64, 0:128],
                   gb(par, 384, 64), gb(par, 0, 128), start=True, stop=False)
                mm(p, ps[2 + par][0:64, 0:128], carb[0:64, :], gb(par, 256, 128)[0:64, :],
                   start=False, stop=True)
                pey = p.step("pe")
                pe_done2[("Y", 0)] = pey
                pe_done2[("grp", par)] = pey
                p.wait_eng("act", "pe", pey)
                act(p, Yf[r0:r0 + 64, oc, tile * 128:(tile + 1) * 128], ps[2 + par][0:64, 0:128],
                    AF.Copy, scale=1.0, bias=0.0)
                ac_done[("Y",)] = p.step("act")
                pcol = Pbc[0:64, tile * 16 + h: tile * 16 + h + 1]
                p.wait_eng("dve", "pe", pey)
                tt(p, "dve", ga(par, 320, 64)[0:64, :], carry[0:64, :], pcol.to_broadcast([64, 64]), ALU.mult)
                p.raw("dve", lambda E: E.drain())
                p.wait_eng("dve", "act", ac_done[("S", par)])
                tt(p, "dve", carry[0:64, :], ga(par, 320, 64)[0:64, :], ga(par, 256, 64)[0:64, :], ALU.add)
                p.raw("dve", lambda E: E.drain())
                dvc = p.step("dve")
                dv_done[("grp", par)] = dvc
                dv_done[("Sc", par)] = dvc
    p.barrier()
    if stop_after == "q2":
        xob = x_out.bitcast(BF16)
        for c in range(NCH):
            p.dma("act", xob[c * 128:(c + 1) * 128, 0:NT], Yf[:, c, :])
        return

    for oc in range(NCH):
        ld = p.dma("sp", U16[:, 8192:12288], zs_d[oc].rearrange("p t -> p t"))
        p.wait("dve", "q_sp", ld)
        p.wait_eng("dve", "act", p.cnt["m_act"])
        tt(p, "dve", Yf[:, oc, :], Yf[:, oc, :], U16[:, 8192:12288], ALU.mult)
        dvc = p.step("dve")
        p.wait("sp", "m_dve", dvc)
    p.barrier()
    if stop_after == "q3a":
        xob = x_out.bitcast(BF16)
        for c in range(NCH):
            p.dma("act", xob[c * 128:(c + 1) * 128, 0:NT], Yf[:, c, :])
        return

    wr = load_wblk(w["l1_WoT"])
    p.wait_eng("pe", "dve", wr)
    for tti in range(NTT):
        tok = slice(tti * TTILE, (tti + 1) * TTILE)
        for og in range(2):
            pset = ps[og * 4:(og + 1) * 4]
            p.wait("sp", "q_act", p.cnt["q_act"])
            p.wait("sp", "m_dve", p.cnt["m_dve"])
            lx = {}
            for o in range(4):
                oc = og * 4 + o
                lx[oc] = p.dma("sp", F1[:, o // 2, (o % 2) * 512:(o % 2 + 1) * 512],
                               x_in[oc * 128:(oc + 1) * 128, tok])
            p.wait_eng("pe", "act", p.cnt["m_act"])
            for o in range(4):
                oc = og * 4 + o
                for i in range(NCH):
                    mm(p, pset[o][:], wblk[:, i, oc * 128:(oc + 1) * 128],
                       Yf[:, i, tok], start=(i == 0), stop=(i == NCH - 1))
            pec = p.step("pe")
            p.wait_eng("act", "pe", pec)
            for o in range(4):
                oc = og * 4 + o
                dst = SCR[:, (o % 2) * 512:(o % 2 + 1) * 512]
                p.wait_eng("act", "dve", p.cnt["m_dve"] if o >= 2 else 0)
                act(p, dst, pset[o][:], AF.Copy, scale=1.0, bias=0.0)
                avc = p.step("act")
                p.wait_eng("dve", "act", avc)
                p.wait("dve", "q_sp", lx[oc])
                xs = F1[:, o // 2, (o % 2) * 512:(o % 2 + 1) * 512]
                tt(p, "dve", xs, xs, dst, ALU.add)
                dvc = p.step("dve")
                p.wait("act", "m_dve", dvc)
                p.dma("act", xacc_d[oc * 128:(oc + 1) * 128, tok], xs)
    p.barrier()
    if stop_after == "q3b":
        for c in range(NCH):
            for hh in range(2):
                ld = p.dma("sp", F1[:, hh, :], xacc_d[c * 128:(c + 1) * 128, hh * H2:(hh + 1) * H2])
                p.wait("act", "q_sp", ld)
                p.dma("act", x_out[c * 128:(c + 1) * 128, hh * H2:(hh + 1) * H2], F1[:, hh, :])
                p.wait("sp", "q_act", p.cnt["q_act"])
        return

    sw1b = U16[:, 0:4096].rearrange("p (i n) -> p i n", n=512)
    sw2b = U16[:, 4096:8192].rearrange("p (i n) -> p i n", n=512)
    dv0 = p.cnt["m_dve"]
    for i in range(NCH):
        p.wait("sp", "m_dve", p.cnt["m_dve"])
        ldx = p.dma("sp", SCR[:, 0:512], w["l1_sw1T"][i * 128:(i + 1) * 128, :])
        ldy = p.dma("sp", SM[:, 0:512], w["l1_sw2T"][i * 128:(i + 1) * 128, :])
        p.wait("dve", "q_sp", ldy)
        p.wait_eng("dve", "pe", p.cnt["m_pe"])
        copy(p, "dve", sw1b[:, i, :], SCR[:, 0:512])
        p.raw("dve", lambda E: E.drain())
        copy(p, "dve", sw2b[:, i, :], SM[:, 0:512])
        p.step("dve")
    wr = p.cnt["m_dve"]
    p.wait_eng("pe", "dve", wr)
    lb = p.dma("sp", SM[:, 512:516], w["l1_sb1c"])
    lb = p.dma("sp", SM[:, 516:520], w["l1_sb2c"])
    lb = p.dma("sp", SM[:, 520:524], w["l1_slng"])
    lb = p.dma("sp", SM[:, 524:528], w["l1_slnb"])
    p.wait("act", "q_sp", lb)
    p.wait("dve", "q_sp", lb)
    for tti in range(NTT):
        tok = slice(tti * TTILE, (tti + 1) * TTILE)
        p.wait_eng("pe", "act", p.cnt["m_act"])
        p.wait_eng("pe", "dve", p.cnt["m_dve"])
        for hc in range(4):
            for i in range(NCH):
                mm(p, ps[hc][:], sw1b[:, i, hc * 128:(hc + 1) * 128], xn[:, i, tok],
                   start=(i == 0), stop=(i == NCH - 1))
            for i in range(NCH):
                mm(p, ps[4 + hc][:], sw2b[:, i, hc * 128:(hc + 1) * 128], xn[:, i, tok],
                   start=(i == 0), stop=(i == NCH - 1))
        pec = p.step("pe")
        p.wait_eng("act", "pe", pec)
        shv = U32[:, 0:2048].rearrange("p (a b) -> p a b", b=512)
        s2v = U32[:, 2048:4096].rearrange("p (a b) -> p a b", b=512)
        for hc in range(4):
            act(p, shv[:, hc, :], ps[hc][:], AF.Silu, bias=SM[:, 512 + hc:513 + hc])
            p.raw("act", lambda E: E.drain())
            act(p, s2v[:, hc, :], ps[4 + hc][:], AF.Identity, bias=SM[:, 516 + hc:517 + hc])
            p.raw("act", lambda E: E.drain())
        avc = p.step("act")
        p.wait_eng("dve", "act", avc)
        tt(p, "dve", U32[:, 0:2048], U32[:, 0:2048], U32[:, 2048:4096], ALU.mult)
        p.raw("dve", lambda E: E.drain())
        copy(p, "dve", U16[:, 8192:10240], U32[:, 0:2048])
        dvc = p.step("dve")
        shb = U16[:, 8192:10240].rearrange("p (a b) -> p a b", b=512)
        p.wait_eng("act", "dve", dvc)
        for hc in range(4):
            p.wait_eng("act", "pe", p.cnt["m_pe"])
            act(p, QT[:, 0:512], shb[:, hc, :], AF.Square)
            avq = p.step("act")
            p.wait_eng("pe", "dve", dvc)
            p.wait_eng("pe", "act", avq)
            mm(p, ps[0][:], ones128[:], shb[:, hc, :], start=(hc == 0), stop=(hc == 3))
            mm(p, ps[1][:], ones128[:], QT[:, 0:512], start=(hc == 0), stop=(hc == 3))
            pec = p.step("pe")
        p.wait_eng("act", "pe", pec)
        act(p, SM[:, 528:1024][:, 0:496][:, 0:512] if False else SCR[:, 0:512], ps[0][:], AF.Copy, scale=1.0 / 512, bias=0.0)
        p.raw("act", lambda E: E.drain())
        act(p, SCR[:, 512:1024], ps[1][:], AF.Copy, scale=1.0 / 512, bias=0.0)
        avc = p.step("act")
        p.wait_eng("dve", "act", avc)
        mu2 = U32[:, 4096:4608]
        tt(p, "dve", mu2, SCR[:, 0:512], SCR[:, 0:512], ALU.mult)
        p.raw("dve", lambda E: E.drain())
        tt(p, "dve", SCR[:, 512:1024], SCR[:, 512:1024], mu2, ALU.subtract)
        dvc = p.step("dve")
        p.wait_eng("act", "dve", dvc)
        act(p, mu2, SCR[:, 512:1024], AF.Sqrt, bias=cvec[:, 0:1])
        avc = p.step("act")
        p.wait_eng("dve", "act", avc)
        recip(p, SCR[:, 512:1024], mu2)
        p.raw("dve", lambda E: E.drain())
        for hc in range(4):
            A = U32[:, 4608:5120]
            Bv = U32[:, 5120:5632]
            ts1(p, "dve", A, SCR[:, 512:1024], SM[:, 520 + hc:521 + hc], ALU.mult)
            p.raw("dve", lambda E: E.drain())
            tt(p, "dve", Bv, SCR[:, 0:512], A, ALU.mult)
            p.raw("dve", lambda E: E.drain())
            ts2(p, "dve", Bv, Bv, -1.0, SM[:, 524 + hc:525 + hc], ALU.mult, ALU.add)
            p.raw("dve", lambda E: E.drain())
            tt(p, "dve", U32[:, 5632:6144], shb[:, hc, :], A, ALU.mult)
            p.raw("dve", lambda E: E.drain())
            p.wait("dve", "q_act", max(0, p.cnt["q_act"] - 16))
            tt(p, "dve", U16[:, 10240 + hc * 512:10752 + hc * 512], U32[:, 5632:6144], Bv, ALU.add)
            dvc = p.step("dve")
            p.wait("act", "m_dve", dvc)
            p.dma("act", sh_d[hc, :, tok], U16[:, 10240 + hc * 512:10752 + hc * 512])
    p.barrier()
    if stop_after == "q3c":
        xob = x_out.bitcast(BF16)
        for hc in range(4):
            ld = p.dma("sp", U16[:, 0:4096], sh_d[hc])
            p.wait("act", "q_sp", ld)
            st = p.dma("act", xob[hc * 128:(hc + 1) * 128, 0:4096], U16[:, 0:4096])
            p.wait("sp", "q_act", st)
        return

    for g in range(4):
        from concourse.bass_core_rust import IndirectOffsetOnAxis
        stg = U16[:, 0:4096].rearrange("p (a b) -> p a b", b=1024)
        p.raw("dve", lambda E: E.memset(U16[:, 0:4096], 0.0))
        dvz = p.step("dve")
        p.wait("pool", "m_dve", dvz)
        for j in range(4):
            sem = p.sems["q_pool"]
            p.cnt["q_pool"] += 16
            def _f(E, j=j, sem=sem, g=g):
                E.indirect_dma_start(
                    out=stg[:, j, :], out_offset=None, in_=xn_tm_d,
                    in_offset=IndirectOffsetOnAxis(ap=idxs[:, g * 4 + j:g * 4 + j + 1], axis=0),
                    oob_is_err=False, bounds_check=NT - 1).then_inc(sem, 16)
            p.streams["pool"].append(_f)
        gq = p.cnt["q_pool"]
        p.wait("pe", "q_pool", gq)
        p.barrier()
        xmoef = U16[:, 4096:8192].rearrange("p (i n) -> p i n", n=MPAD)
        for j in range(4):
            for c in range(NCH):
                p.wait_eng("pe", "act", p.cnt["m_act"])
                p.raw("pe", lambda E, j=j, c=c: E.transpose(out=pst, in_=stg[:, j, c * 128:(c + 1) * 128],
                                                            identity=ident[:]))
                pec = p.step("pe")
                p.wait_eng("act", "pe", pec)
                act(p, xmoef[:, c, j * 128:(j + 1) * 128], pst, AF.Copy, scale=1.0, bias=0.0)
                p.step("act")
        p.barrier()
        if stop_after == "q3d0":
            xob = x_out.bitcast(BF16)
            for c in range(NCH):
                p.dma("act", xob[c * 128:(c + 1) * 128, 0:MPAD], xmoef[:, c, :])
            return

        gwb = U16[:, 8192:8448].rearrange("p (i n) -> p i n", n=32)
        lg = p.dma("sp", SCR[:, 0:256].rearrange("p (i n) -> p i n", n=32),
                   w["l1_gwpack"].rearrange("(i p) n -> p i n", p=128))
        p.wait("dve", "q_sp", lg)
        copy(p, "dve", U16[:, 8192:8448], SCR[:, 0:256])
        dvc = p.step("dve")
        p.wait_eng("pe", "dve", dvc)
        for j in range(4):
            pa = ps[j % 2]
            p.wait_eng("pe", "act", p.cnt["m_act"] if j >= 2 else 0)
            for i in range(NCH):
                mm(p, pa[:, 0:32], xmoef[:, i, j * 128:(j + 1) * 128], gwb[:, i, :],
                   start=(i == 0), stop=(i == NCH - 1))
            pec = p.step("pe")
            p.wait_eng("act", "pe", pec)
            act(p, U32[:, j * 32:(j + 1) * 32], pa[:, 0:32], AF.Copy, scale=1.0, bias=0.0)
            p.step("act")
        avc = p.cnt["m_act"]
        p.wait_eng("dve", "act", avc)
        scores = U32[:, 0:128].rearrange("p (a b) -> p a b", b=32)
        s2 = U32[:, 128:256].rearrange("p (a b) -> p a b", b=32)
        tt(p, "dve", scores, scores, rep_mid(gbias_bc[:], 4), ALU.add)
        dvc = p.step("dve")
        p.wait_eng("act", "dve", dvc)
        act(p, U32[:, 0:128], U32[:, 0:128], AF.Sigmoid)
        avc = p.step("act")
        p.wait_eng("dve", "act", avc)
        tt(p, "dve", s2, scores, rep_mid(ebias_bc[:], 4), ALU.add)
        p.raw("dve", lambda E: E.drain())
        rmax = U32[:, 256:260]
        p.raw("dve", lambda E: E.tensor_reduce(out=rmax.rearrange("p a -> p a 1" ) if False else U32[:, 256:260],
                                               in_=s2[:, :, 0:8], axis=mybir.AxisListType.X, op=ALU.max))
        p.raw("dve", lambda E: E.drain())
        maskv = U32[:, 384:416].rearrange("p (a b) -> p a b", b=8)
        tt(p, "dve", maskv, s2[:, :, 0:8], rep_in(rmax, 8), ALU.is_ge)
        p.raw("dve", lambda E: E.drain())
        t1s = U32[:, 416:448].rearrange("p (a b) -> p a b", b=8)
        tt(p, "dve", t1s, maskv, scores[:, :, 0:8], ALU.mult)
        p.raw("dve", lambda E: E.drain())
        top1 = U32[:, 448:452]
        p.raw("dve", lambda E: E.tensor_reduce(out=top1, in_=t1s, axis=mybir.AxisListType.X, op=ALU.add))
        p.raw("dve", lambda E: E.drain())
        shs = U32[:, 452:456]
        copy(p, "dve", shs, scores[:, :, 8:9].rearrange("p a b -> p (a b)"))
        p.raw("dve", lambda E: E.drain())
        mx2 = U32[:, 456:460]
        tt(p, "dve", mx2, shs, top1, ALU.max)
        p.raw("dve", lambda E: E.drain())
        tt(p, "dve", U32[:, 460:464], shs, mx2, ALU.subtract)
        p.raw("dve", lambda E: E.drain())
        tt(p, "dve", U32[:, 464:468], top1, mx2, ALU.subtract)
        dvc = p.step("dve")
        p.wait_eng("act", "dve", dvc)
        act(p, U32[:, 468:472], U32[:, 460:464], AF.Exp)
        p.raw("act", lambda E: E.drain())
        act(p, U32[:, 472:476], U32[:, 464:468], AF.Exp)
        avc = p.step("act")
        p.wait_eng("dve", "act", avc)
        tt(p, "dve", U32[:, 476:480], U32[:, 468:472], U32[:, 472:476], ALU.add)
        p.raw("dve", lambda E: E.drain())
        recip(p, U32[:, 480:484], U32[:, 476:480])
        p.raw("dve", lambda E: E.drain())
        tt(p, "dve", U32[:, 484:488], U32[:, 468:472], U32[:, 480:484], ALU.mult)
        p.raw("dve", lambda E: E.drain())
        tt(p, "dve", U32[:, 488:492], U32[:, 472:476], U32[:, 480:484], ALU.mult)
        p.raw("dve", lambda E: E.drain())
        ts1(p, "dve", U32[:, 492:496], U32[:, 484:488], 1.0, ALU.subtract)
        dvc = p.step("dve")
        pstf = ps[6][:, 0:128]
        p.wait_eng("pe", "dve", dvc)
        p.raw("pe", lambda E: E.transpose(out=pstf[0:8, :], in_=U32[:, 488:496], identity=identf[:]))
        pec = p.step("pe")
        p.wait_eng("act", "pe", pec)
        act(p, SCR[0:8, 0:128], pstf[0:8, :], AF.Copy, scale=1.0, bias=0.0)
        avt = p.step("act")
        p.dma("act", gat_d[g], SCR[0:8, 0:128])
        p.wait_eng("pe", "act", avt)
        p.raw("pe", lambda E: E.transpose(out=pstf[0:32, :], in_=U32[:, 384:416], identity=identf[:]))
        pec = p.step("pe")
        p.wait_eng("act", "pe", pec)
        act(p, SCR[0:32, 128:256], pstf[0:32, :], AF.Copy, scale=1.0, bias=0.0)
        avt = p.step("act")
        st = p.dma("act", mask_d[g], SCR[0:32, 128:256])
        p.wait("sp", "q_act", st)
        p.barrier()
        if stop_after == "q3d1":
            p.dma("act", x_out[0:128, 0:512], U32[:, 0:512])
            return

        corr = U32[:, 4096:8192].rearrange("p (a b) -> p a b", b=512)
        ew1v = U16[:, 0:4096].rearrange("p (i n) -> p i n", n=512)
        ew2v = U16[:, 8192:12288].rearrange("p (i n) -> p i n", n=512)
        ew3v = U16[:, 8192:12288].rearrange("p (i n) -> p i n", n=1024)
        zprv = U16[:, 12288:14336].rearrange("p (a b) -> p a b", b=512)
        zmv = U16[:, 14336:16384].rearrange("p (a b) -> p a b", b=512)
        maskEb = QT[:, 512:1024]

        lbe = p.dma("sp", SM[:, 0:32], w["l1_eb1c"])
        lbe = p.dma("sp", SM[:, 32:64], w["l1_eb2c"])
        lbe = p.dma("sp", SM[:, 64:68], w["l1_selng"])
        lbe = p.dma("sp", SM[:, 68:72], w["l1_selnb"])
        p.wait("act", "q_sp", lbe)
        p.wait("dve", "q_sp", lbe)
        for e in range(8):
            for i in range(NCH):
                p.wait("sp", "m_dve", p.cnt["m_dve"])
                l1x = p.dma("sp", SCR[:, 0:512], w["l1_ew1T"][e, i * 128:(i + 1) * 128, :])
                l2x = p.dma("sp", SM[:, 512:1024], w["l1_ew2T"][e, i * 128:(i + 1) * 128, :])
                p.wait("dve", "q_sp", l2x)
                p.wait_eng("dve", "pe", p.cnt["m_pe"])
                copy(p, "dve", ew1v[:, i, :], SCR[:, 0:512])
                p.raw("dve", lambda E: E.drain())
                copy(p, "dve", ew2v[:, i, :], SM[:, 512:1024])
                p.step("dve")
            wr = p.cnt["m_dve"]
            p.wait_eng("pe", "dve", wr)
            p.wait_eng("pe", "act", p.cnt["m_act"])
            for hc in range(4):
                for i in range(NCH):
                    mm(p, ps[hc][:], ew1v[:, i, hc * 128:(hc + 1) * 128], xmoef[:, i, :],
                       start=(i == 0), stop=(i == NCH - 1))
                for i in range(NCH):
                    mm(p, ps[4 + hc][:], ew2v[:, i, hc * 128:(hc + 1) * 128], xmoef[:, i, :],
                       start=(i == 0), stop=(i == NCH - 1))
            pec = p.step("pe")
            p.wait_eng("act", "pe", pec)
            h1v = U32[:, 0:2048].rearrange("p (a b) -> p a b", b=512)
            h2v = U32[:, 1536:3584][:, 512:2560] if False else None
            h2v = U32[:, 2048:4096].rearrange("p (a b) -> p a b", b=512)
            for hc in range(4):
                act(p, h1v[:, hc, :], ps[hc][:], AF.Silu, bias=SM[:, e * 4 + hc:e * 4 + hc + 1])
                p.raw("act", lambda E: E.drain())
                act(p, h2v[:, hc, :], ps[4 + hc][:], AF.Identity, bias=SM[:, 32 + e * 4 + hc:32 + e * 4 + hc + 1])
                p.raw("act", lambda E: E.drain())
            avc = p.step("act")
            p.wait_eng("dve", "act", avc)
            tt(p, "dve", U32[:, 0:2048], U32[:, 0:2048], U32[:, 2048:4096], ALU.mult)
            p.raw("dve", lambda E: E.drain())
            copy(p, "dve", U16[:, 12288:14336], U32[:, 0:2048])
            dvc = p.step("dve")
            p.wait_eng("act", "dve", dvc)
            for hc in range(4):
                p.wait_eng("act", "pe", p.cnt["m_pe"])
                act(p, QT[:, 0:512], zprv[:, hc, :], AF.Square)
                avq = p.step("act")
                p.wait_eng("pe", "dve", dvc)
                p.wait_eng("pe", "act", avq)
                mm(p, ps[0][:], ones128[:], zprv[:, hc, :], start=(hc == 0), stop=(hc == 3))
                mm(p, ps[1][:], ones128[:], QT[:, 0:512], start=(hc == 0), stop=(hc == 3))
                pec = p.step("pe")
            p.wait_eng("act", "pe", pec)
            act(p, SCR[:, 0:512], ps[0][:], AF.Copy, scale=1.0 / 512, bias=0.0)
            p.raw("act", lambda E: E.drain())
            act(p, SCR[:, 512:1024], ps[1][:], AF.Copy, scale=1.0 / 512, bias=0.0)
            avc = p.step("act")
            p.wait_eng("dve", "act", avc)
            mu2 = U32[:, 0:512]
            tt(p, "dve", mu2, SCR[:, 0:512], SCR[:, 0:512], ALU.mult)
            p.raw("dve", lambda E: E.drain())
            tt(p, "dve", SCR[:, 512:1024], SCR[:, 512:1024], mu2, ALU.subtract)
            dvc = p.step("dve")
            p.wait_eng("act", "dve", dvc)
            act(p, mu2, SCR[:, 512:1024], AF.Sqrt, bias=cvec[:, 0:1])
            avc = p.step("act")
            p.wait_eng("dve", "act", avc)
            recip(p, SCR[:, 512:1024], mu2)
            p.raw("dve", lambda E: E.drain())
            lm = p.dma("sp", SMALL[:, 3584:4096].rearrange("p (a b) -> p a b", b=128),
                       mask_d[g].rearrange("(a e) p -> e a p", e=8)[e].partition_broadcast(128))
            p.wait("dve", "q_sp", lm)
            for hc in range(4):
                A = U32[:, 1536:2048]
                Bv = U32[:, 2048:2560]
                ts1(p, "dve", A, SCR[:, 512:1024], SM[:, 64 + hc:65 + hc], ALU.mult)
                p.raw("dve", lambda E: E.drain())
                tt(p, "dve", Bv, SCR[:, 0:512], A, ALU.mult)
                p.raw("dve", lambda E: E.drain())
                ts2(p, "dve", Bv, Bv, -1.0, SM[:, 68 + hc:69 + hc], ALU.mult, ALU.add)
                p.raw("dve", lambda E: E.drain())
                tt(p, "dve", U32[:, 2560:3072], zprv[:, hc, :], A, ALU.mult)
                p.raw("dve", lambda E: E.drain())
                tt(p, "dve", U32[:, 2560:3072], U32[:, 2560:3072], Bv, ALU.add)
                p.raw("dve", lambda E: E.drain())
                tt(p, "dve", zmv[:, hc, :], U32[:, 2560:3072], SMALL[:, 3584:4096], ALU.mult)
                dvh = p.step("dve")
            p.wait("sp", "m_dve", dvh)
            dvz2 = p.cnt["m_dve"]
            if stop_after == "q3dz" and g == 0 and e == 0:
                p.barrier()
                p.dma("act", x_out[0:128, 0:2048], U16[:, 12288:16384].bitcast(F32))
                p.dma("act", x_out[128:256, 0:512], U32[:, 3072:3584])
                p.dma("act", x_out[256:384, 0:512], U32[:, 2560:3072])
                return
            for hc in range(4):
                p.wait("sp", "m_dve", p.cnt["m_dve"])
                p.wait("sp", "m_pe", p.cnt["m_pe"])
                l3x = p.dma("sp", SCR[:, 0:512], w["l1_ew3T"][e, hc * 128:(hc + 1) * 128, 0:512])
                l3y = p.dma("sp", SM[:, 512:1024], w["l1_ew3T"][e, hc * 128:(hc + 1) * 128, 512:1024])
                p.wait("dve", "q_sp", l3y)
                p.wait_eng("dve", "pe", p.cnt["m_pe"])
                copy(p, "dve", ew3v[:, hc, 0:512], SCR[:, 0:512])
                p.raw("dve", lambda E: E.drain())
                copy(p, "dve", ew3v[:, hc, 512:1024], SM[:, 512:1024])
                p.step("dve")
            wr3 = p.cnt["m_dve"]
            p.wait_eng("pe", "dve", wr3)
            for oc in range(NCH):
                pa = ps[oc % 2]
                p.wait_eng("pe", "dve", p.cnt["m_dve"])
                for hc in range(4):
                    mm(p, pa[:], ew3v[:, hc, oc * 128:(oc + 1) * 128],
                       zmv[:, hc, :], start=(hc == 0), stop=(hc == 3))
                pec = p.step("pe")
                p.wait_eng("act", "pe", pec)
                p.wait_eng("act", "dve", p.cnt["m_dve"])
                act(p, U32[:, 2560:3072], pa[:], AF.Copy, scale=1.0, bias=0.0)
                avc = p.step("act")
                p.wait_eng("dve", "act", avc)
                if e == 0:
                    copy(p, "dve", corr[:, oc, :], U32[:, 2560:3072])
                else:
                    tt(p, "dve", corr[:, oc, :], corr[:, oc, :], U32[:, 2560:3072], ALU.add)
                p.step("dve")
        p.barrier()
        if stop_after == "q3e0" and g == 0:
            p.dma("act", x_out[0:128, 0:4096], U32[:, 4096:8192])
            return

        wshm1_bc = SMALL[:, 2560:3072]
        wexp_bc = SMALL[:, 3072:3584]
        ld = p.dma("sp", wshm1_bc.rearrange("p (a b) -> p a b", b=128),
                   gat_d[g].rearrange("(w a) p -> w a p", w=2)[1].partition_broadcast(128))
        ld = p.dma("sp", wexp_bc.rearrange("p (a b) -> p a b", b=128),
                   gat_d[g].rearrange("(w a) p -> w a p", w=2)[0].partition_broadcast(128))
        p.wait("dve", "q_sp", ld)
        for oc in range(NCH):
            tt(p, "dve", corr[:, oc, :], corr[:, oc, :], wexp_bc, ALU.mult)
            p.step("dve")
        sw1b2 = ew1v
        sw2b2 = ew2v.rearrange("p i n -> p i n") if False else U16[:, 8192:12288].rearrange("p (i n) -> p i n", n=512)
        for i in range(NCH):
            p.wait("sp", "m_dve", p.cnt["m_dve"])
            l1x = p.dma("sp", SCR[:, 0:512], w["l1_sw1T"][i * 128:(i + 1) * 128, :])
            l2x = p.dma("sp", SM[:, 512:1024], w["l1_sw2T"][i * 128:(i + 1) * 128, :])
            p.wait("dve", "q_sp", l2x)
            p.wait_eng("dve", "pe", p.cnt["m_pe"])
            copy(p, "dve", sw1b2[:, i, :], SCR[:, 0:512])
            p.raw("dve", lambda E: E.drain())
            copy(p, "dve", sw2b2[:, i, :], SM[:, 512:1024])
            p.step("dve")
        wr = p.cnt["m_dve"]
        lb = p.dma("sp", SM[:, 0:4], w["l1_sb1c"])
        lb = p.dma("sp", SM[:, 4:8], w["l1_sb2c"])
        lb = p.dma("sp", SM[:, 8:12], w["l1_slng"])
        lb = p.dma("sp", SM[:, 12:16], w["l1_slnb"])
        p.wait("act", "q_sp", lb)
        p.wait_eng("pe", "dve", wr)
        p.wait_eng("pe", "act", p.cnt["m_act"])
        for hc in range(4):
            for i in range(NCH):
                mm(p, ps[hc][:], sw1b2[:, i, hc * 128:(hc + 1) * 128], xmoef[:, i, :],
                   start=(i == 0), stop=(i == NCH - 1))
            for i in range(NCH):
                mm(p, ps[4 + hc][:], sw2b2[:, i, hc * 128:(hc + 1) * 128], xmoef[:, i, :],
                   start=(i == 0), stop=(i == NCH - 1))
        pec = p.step("pe")
        p.wait_eng("act", "pe", pec)
        h1v = U32[:, 1024:3072].rearrange("p (a b) -> p a b", b=512)
        h2v = U32[:, 3072:3584]
        for hc in range(4):
            act(p, h1v[:, hc, :], ps[hc][:], AF.Silu, bias=SM[:, 0 + hc:1 + hc])
            p.raw("act", lambda E: E.drain())
            act(p, U32[:, 3072:3584], ps[4 + hc][:], AF.Identity, bias=SM[:, 4 + hc:5 + hc])
            avc = p.step("act")
            p.wait_eng("dve", "act", avc)
            tt(p, "dve", h1v[:, hc, :], h1v[:, hc, :], U32[:, 3072:3584], ALU.mult)
            dvc = p.step("dve")
            p.wait_eng("act", "dve", dvc)
        p.wait_eng("dve", "act", p.cnt["m_act"])
        copy(p, "dve", U16[:, 12288:14336], U32[:, 1024:3072])
        dvc = p.step("dve")
        p.wait_eng("act", "dve", dvc)
        for hc in range(4):
            p.wait_eng("act", "pe", p.cnt["m_pe"])
            act(p, QT[:, 0:512], zprv[:, hc, :], AF.Square)
            avq = p.step("act")
            p.wait_eng("pe", "dve", dvc)
            p.wait_eng("pe", "act", avq)
            mm(p, ps[0][:], ones128[:], zprv[:, hc, :], start=(hc == 0), stop=(hc == 3))
            mm(p, ps[1][:], ones128[:], QT[:, 0:512], start=(hc == 0), stop=(hc == 3))
            pec = p.step("pe")
        p.wait_eng("act", "pe", pec)
        act(p, SCR[:, 0:512], ps[0][:], AF.Copy, scale=1.0 / 512, bias=0.0)
        p.raw("act", lambda E: E.drain())
        act(p, SCR[:, 512:1024], ps[1][:], AF.Copy, scale=1.0 / 512, bias=0.0)
        avc = p.step("act")
        p.wait_eng("dve", "act", avc)
        mu2 = U32[:, 1024:1536]
        tt(p, "dve", mu2, SCR[:, 0:512], SCR[:, 0:512], ALU.mult)
        p.raw("dve", lambda E: E.drain())
        tt(p, "dve", SCR[:, 512:1024], SCR[:, 512:1024], mu2, ALU.subtract)
        dvc = p.step("dve")
        p.wait_eng("act", "dve", dvc)
        act(p, mu2, SCR[:, 512:1024], AF.Sqrt, bias=cvec[:, 0:1])
        avc = p.step("act")
        p.wait_eng("dve", "act", avc)
        recip(p, SCR[:, 512:1024], mu2)
        p.raw("dve", lambda E: E.drain())
        shn = U16[:, 14336:16384].rearrange("p (a b) -> p a b", b=512)
        for hc in range(4):
            A = U32[:, 1024:1536]
            Bv = U32[:, 1536:2048]
            ts1(p, "dve", A, SCR[:, 512:1024], SM[:, 8 + hc:9 + hc], ALU.mult)
            p.raw("dve", lambda E: E.drain())
            tt(p, "dve", Bv, SCR[:, 0:512], A, ALU.mult)
            p.raw("dve", lambda E: E.drain())
            ts2(p, "dve", Bv, Bv, -1.0, SM[:, 12 + hc:13 + hc], ALU.mult, ALU.add)
            p.raw("dve", lambda E: E.drain())
            tt(p, "dve", U32[:, 2048:2560], zprv[:, hc, :], A, ALU.mult)
            p.raw("dve", lambda E: E.drain())
            tt(p, "dve", shn[:, hc, :], U32[:, 2048:2560], Bv, ALU.add)
            p.step("dve")
        dvsh = p.cnt["m_dve"]
        sw3b = U16[:, 8192:12288].rearrange("p (i n) -> p i n", n=1024)
        for hc in range(4):
            p.wait("sp", "m_dve", p.cnt["m_dve"])
            p.wait("sp", "m_pe", p.cnt["m_pe"])
            l3x = p.dma("sp", SCR[:, 0:512], w["l1_sw3T"][hc * 128:(hc + 1) * 128, 0:512])
            l3y = p.dma("sp", SM[:, 512:1024], w["l1_sw3T"][hc * 128:(hc + 1) * 128, 512:1024])
            p.wait("dve", "q_sp", l3y)
            p.wait_eng("dve", "pe", p.cnt["m_pe"])
            copy(p, "dve", sw3b[:, hc, 0:512], SCR[:, 0:512])
            p.raw("dve", lambda E: E.drain())
            copy(p, "dve", sw3b[:, hc, 512:1024], SM[:, 512:1024])
            p.step("dve")
        wr3 = p.cnt["m_dve"]
        lb3 = p.dma("sp", SM[:, 16:24], w["l1_sb3c"])
        p.wait("act", "q_sp", lb3)
        p.wait_eng("pe", "dve", max(wr3, dvsh))
        eb3b = U16[:, 0:1024]
        mkb = U16[:, 1024:1536]
        p.wait("sp", "m_dve", p.cnt["m_dve"])
        le = p.dma("sp", SCR[32:40, 0:1024], w["l1_eb3T"])
        lf = p.dma("sp", SM[32:40, 512:1024].rearrange("p (a b) -> p a b", b=128),
                   mask_d[g].rearrange("(a e) p -> e a p", e=8))
        p.wait("dve", "q_sp", lf)
        copy(p, "dve", eb3b[32:40, :], SCR[32:40, 0:1024])
        p.raw("dve", lambda E: E.drain())
        copy(p, "dve", mkb[32:40, 0:512], SM[32:40, 512:1024])
        dveb = p.step("dve")
        p.wait_eng("pe", "dve", dveb)
        for oc in range(NCH):
            pa = ps[oc % 2]
            pb = ps[2 + oc % 2]
            p.wait_eng("pe", "dve", p.cnt["m_dve"])
            for hc in range(4):
                mm(p, pa[:], sw3b[:, hc, oc * 128:(oc + 1) * 128], shn[:, hc, :],
                   start=(hc == 0), stop=(hc == 3))
            mm(p, pb[:], eb3b[32:40, oc * 128:(oc + 1) * 128], mkb[32:40, 0:512],
               start=True, stop=True)
            pec = p.step("pe")
            p.wait_eng("act", "pe", pec)
            p.wait_eng("act", "dve", p.cnt["m_dve"])
            act(p, U32[:, 2048:2560], pa[:], AF.Identity, bias=SM[:, 16 + oc:17 + oc])
            p.raw("act", lambda E: E.drain())
            act(p, U32[:, 2560:3072], pb[:], AF.Copy, scale=1.0, bias=0.0)
            avc = p.step("act")
            p.wait_eng("dve", "act", avc)
            tt(p, "dve", U32[:, 2560:3072], U32[:, 2560:3072], wexp_bc, ALU.mult)
            p.raw("dve", lambda E: E.drain())
            tt(p, "dve", corr[:, oc, :], corr[:, oc, :], U32[:, 2560:3072], ALU.add)
            p.raw("dve", lambda E: E.drain())
            tt(p, "dve", U32[:, 2048:2560], U32[:, 2048:2560], wshm1_bc, ALU.mult)
            p.raw("dve", lambda E: E.drain())
            tt(p, "dve", corr[:, oc, :], corr[:, oc, :], U32[:, 2048:2560], ALU.add)
            p.step("dve")
        p.barrier()
        if stop_after == "q3e1" and g == 0:
            p.dma("act", x_out[0:128, 0:4096], U32[:, 4096:8192])
            return

        corr_tm = U16[:, 4096:8192].rearrange("p (a b) -> p a b", b=1024)
        cb = U16[:, 12288:16384].rearrange("p (a b) -> p a b", b=512)
        p.wait_eng("dve", "pe", p.cnt["m_pe"])
        copy(p, "dve", U16[:, 12288:16384], U32[:, 4096:8192])
        dvc = p.step("dve")
        for oc in range(NCH):
            for j4 in range(4):
                p.wait_eng("pe", "dve", dvc)
                p.wait_eng("pe", "act", p.cnt["m_act"])
                p.raw("pe", lambda E, oc=oc, j4=j4: E.transpose(
                    out=pst[:], in_=cb[:, oc, j4 * 128:(j4 + 1) * 128], identity=ident[:]))
                pec = p.step("pe")
                p.wait_eng("act", "pe", pec)
                act(p, corr_tm[:, j4, oc * 128:(oc + 1) * 128], pst, AF.Copy, scale=1.0, bias=0.0)
                p.step("act")
        for j4 in range(4):
            p.dma("act", corrtm_d[g * 4 + j4], corr_tm[:, j4, :])
        p.barrier()
    if stop_after == "q3f0":
        xob = x_out.bitcast(BF16)
        for gt in range(16):
            ld = p.dma("sp", U16[:, 0:1024], corrtm_d[gt])
            p.wait("act", "q_sp", ld)
            st = p.dma("act", xob[(gt % 8) * 128:(gt % 8 + 1) * 128,
                                  (gt // 8) * 1024:(gt // 8 + 1) * 1024], U16[:, 0:1024])
            p.wait("sp", "q_act", st)
        return
    sw3c = U16[:, 8192:12288].rearrange("p (i n) -> p i n", n=1024)
    shstr = U16[:, 12288:14336].rearrange("p (a b) -> p a b", b=512)
    crst = U16[:, 14336:16384].rearrange("p (a b) -> p a b", b=128)
    SelT = U16[:, 0:8192].rearrange("p (a b) -> p a b", b=512)
    p.wait_eng("pool", "dve", p.cnt["m_dve"])
    p.raw("pool", lambda E: E.iota(out=SMALL[:, 0:4096], pattern=[[1, 4096]], base=0,
                                   channel_multiplier=0, allow_small_or_imprecise_dtypes=True))
    plc = p.step("pool")
    p.wait_eng("dve", "pool", plc)
    for tti in range(NTT):
        tok = slice(tti * TTILE, (tti + 1) * TTILE)
        p.wait_eng("dve", "pe", p.cnt["m_pe"])
        for gj in range(16):
            ts1(p, "dve", SelT[:, gj, :], SMALL[:, tok], poscol[:, gj:gj + 1], ALU.is_equal)
        dvsel = p.step("dve")
        p.wait("sp", "m_pe", p.cnt["m_pe"])
        lsh = {}
        for hc in range(4):
            lsh[hc] = p.dma("sp", shstr[:, hc, :], sh_d[hc, :, tok])
        lx = {}
        p.wait("sp", "q_act", p.cnt["q_act"])
        p.wait("sp", "m_dve", p.cnt["m_dve"] - 1)
        for oc in range(NCH):
            lx[oc] = p.dma("sp", U32[:, oc * 512:(oc + 1) * 512], xacc_d[oc * 128:(oc + 1) * 128, tok])
        for og in range(2):
            pset = ps[og * 4:(og + 1) * 4]
            p.wait_eng("pe", "act", p.cnt["m_act"] if (tti + og) > 0 else 0)
            p.wait_eng("pe", "dve", dvsel)
            for o in range(4):
                oc = og * 4 + o
                p.wait("sp", "m_pe", p.cnt["m_pe"])
                lcr = p.dma("sp", crst,
                            corrtm_d[0:16, :, oc * 128:(oc + 1) * 128].rearrange("g p f -> p g f"))
                p.wait("pe", "q_sp", max(lcr, lsh[3]))
                for hc in range(4):
                    mm(p, pset[o][:], sw3c[:, hc, oc * 128:(oc + 1) * 128], shstr[:, hc, :],
                       start=(hc == 0), stop=False)
                for gj in range(16):
                    mm(p, pset[o][:], crst[:, gj, :], SelT[:, gj, :],
                       start=False, stop=(gj == 15))
                p.step("pe")
            pec = p.cnt["m_pe"]
            p.wait_eng("act", "pe", pec)
            for o in range(4):
                oc = og * 4 + o
                dst = SCR[:, (o % 2) * 512:(o % 2 + 1) * 512]
                p.wait_eng("act", "dve", p.cnt["m_dve"] if o >= 2 else 0)
                act(p, dst, pset[o][:], AF.Identity, bias=SM[:, 16 + oc:17 + oc])
                avc = p.step("act")
                p.wait_eng("dve", "act", avc)
                p.wait("dve", "q_sp", lx[oc])
                tt(p, "dve", U32[:, oc * 512:(oc + 1) * 512], U32[:, oc * 512:(oc + 1) * 512], dst, ALU.add)
                dvc = p.step("dve")
                p.wait("act", "m_dve", dvc)
                p.dma("act", x_out[oc * 128:(oc + 1) * 128, tok], U32[:, oc * 512:(oc + 1) * 512])
    p.barrier()


def col_pack(v, n):
    v = np.asarray(v, np.float32)
    return np.ascontiguousarray(v.reshape(n, 128).T)


def l1_host_inputs(inp, core):
    import ml_dtypes
    d = {}
    d["l1_WinT"] = np.ascontiguousarray(np.asarray(inp["L1_attn_W_in"], np.float32).T[:, 0:1024])
    d["l1_WinTz"] = np.ascontiguousarray(np.asarray(inp["L1_attn_W_in"], np.float32).T[:, 1024:2048])
    d["l1_WdtT"] = np.ascontiguousarray(np.asarray(inp["L1_attn_W_dt"], np.float32).T)
    d["l1_bdt"] = np.asarray(inp["L1_attn_b_dt"], np.float32).reshape(1, 16)
    d["l1_WBT"] = np.ascontiguousarray(np.asarray(inp["L1_attn_W_B"], np.float32).T)
    d["l1_WCT"] = np.ascontiguousarray(np.asarray(inp["L1_attn_W_C"], np.float32).T)
    d["l1_WoT"] = np.ascontiguousarray(np.asarray(inp["L1_attn_W_out"], np.float32).T)
    d["l1_Alog"] = np.asarray(inp["L1_attn_A_log"], np.float32).reshape(1, 16)
    d["l1_sw1T"] = np.ascontiguousarray(np.asarray(inp["L1_ffn_sw1"], np.float32).T)
    d["l1_sw2T"] = np.ascontiguousarray(np.asarray(inp["L1_ffn_sw2"], np.float32).T)
    d["l1_sw3T"] = np.ascontiguousarray(np.asarray(inp["L1_ffn_sw3"], np.float32).T)
    d["l1_sb1c"] = col_pack(inp["L1_ffn_sb1"], 4)
    d["l1_sb2c"] = col_pack(inp["L1_ffn_sb2"], 4)
    d["l1_sb3c"] = col_pack(inp["L1_ffn_sb3"], 8)
    d["l1_slng"] = col_pack(inp["L1_ffn_s_ln_g"], 4)
    d["l1_slnb"] = col_pack(inp["L1_ffn_s_ln_b"], 4)
    d["l1_selng"] = col_pack(inp["L1_ffn_sel_ln_g"], 4)
    d["l1_selnb"] = col_pack(inp["L1_ffn_sel_ln_b"], 4)
    gw = np.zeros((1024, 32), np.float32)
    gw[:, 0:8] = np.asarray(inp["L1_ffn_eg_w"], np.float32).T
    gw[:, 8] = np.asarray(inp["L1_ffn_sg_w"], np.float32).reshape(1024)
    d["l1_gwpack"] = gw
    gb = np.zeros((1, 32), np.float32)
    gb[0, 0:8] = np.asarray(inp["L1_ffn_eg_b"], np.float32)
    gb[0, 8] = np.asarray(inp["L1_ffn_sg_b"], np.float32)[0]
    d["l1_gbias"] = gb
    eb = np.zeros((1, 32), np.float32)
    eb[0, 0:8] = np.asarray(inp["L1_ffn_exp_bias"], np.float32)
    d["l1_ebias"] = eb
    d["l1_ew1T"] = np.ascontiguousarray(np.transpose(np.asarray(inp["L1_ffn_ew1"], np.float32), (0, 2, 1)))
    d["l1_ew2T"] = np.ascontiguousarray(np.transpose(np.asarray(inp["L1_ffn_ew2"], np.float32), (0, 2, 1)))
    d["l1_ew3T"] = np.ascontiguousarray(np.transpose(np.asarray(inp["L1_ffn_ew3"], np.float32), (0, 2, 1)))
    eb1 = np.asarray(inp["L1_ffn_eb1"], np.float32)
    eb2 = np.asarray(inp["L1_ffn_eb2"], np.float32)
    d["l1_eb1c"] = np.ascontiguousarray(eb1.reshape(8, 4, 128).transpose(2, 0, 1).reshape(128, 32))
    d["l1_eb2c"] = np.ascontiguousarray(eb2.reshape(8, 4, 128).transpose(2, 0, 1).reshape(128, 32))
    d["l1_eb3T"] = np.ascontiguousarray(np.asarray(inp["L1_ffn_eb3"], np.float32))
    ib = np.asarray(inp["L1_ffn_index_b"]).astype(np.int64)
    it = np.asarray(inp["L1_ffn_index_t"]).astype(np.int64)
    own = np.where(ib // 2 == core)[0]
    pos = (ib[own] % 2) * T + it[own]
    pos = np.unique(pos)
    assert pos.size <= 4 * MPAD, f"MoE overflow: {pos.size} > {4 * MPAD}"
    idx = np.full(4 * MPAD, 1 << 24, np.int32)
    idx[:pos.size] = pos.astype(np.int32)
    d["l1_moeidx"] = np.ascontiguousarray(idx.reshape(16, 128).T.copy()).astype(np.int32)
    posf = np.full(4 * MPAD, -1.0, np.float32)
    posf[:pos.size] = pos.astype(np.float32)
    d["l1_moepos"] = np.ascontiguousarray(posf.reshape(16, 128).T.copy())
    return d


def build_launch_l1(stop_after=None):
    p = Prog("l1")
    x_in = p.inp("x1_fm", [D, NT], F32)
    w = {}
    for name, shape, dt in [
        ("l1_WinT", [D, 1024], F32), ("l1_WinTz", [D, 1024], F32),
        ("l1_WdtT", [D, 16], F32), ("l1_bdt", [1, 16], F32),
        ("l1_WBT", [D, D], F32), ("l1_WCT", [D, D], F32), ("l1_WoT", [D, D], F32),
        ("l1_Alog", [1, 16], F32),
        ("l1_sw1T", [D, 512], F32), ("l1_sw2T", [D, 512], F32), ("l1_sw3T", [512, D], F32),
        ("l1_sb1c", [128, 4], F32), ("l1_sb2c", [128, 4], F32), ("l1_sb3c", [128, 8], F32),
        ("l1_slng", [128, 4], F32), ("l1_slnb", [128, 4], F32),
        ("l1_selng", [128, 4], F32), ("l1_selnb", [128, 4], F32),
        ("l1_gwpack", [D, 32], F32), ("l1_gbias", [1, 32], F32), ("l1_ebias", [1, 32], F32),
        ("l1_ew1T", [8, D, 512], F32), ("l1_ew2T", [8, D, 512], F32), ("l1_ew3T", [8, 512, D], F32),
        ("l1_eb1c", [128, 32], F32), ("l1_eb2c", [128, 32], F32), ("l1_eb3T", [8, D], F32),
        ("l1_moeidx", [128, 16], I32), ("l1_moepos", [128, 16], F32),
        ("cvec", [128, 8], F32), ("ones128", [128, 128], BF16),
        ("triU", [128, 128], BF16), ("ident", [128, 128], BF16), ("identf", [128, 128], F32),
    ]:
        w[name] = p.inp(name, shape, dt)
    x_out = p.outp("x2_fm", [D, NT], F32)
    emit_l1(p, x_in, x_out, w, stop_after=stop_after)
    p.finalize()
    return p



NTG = NCORE * NT
NWIN = NCORE


def quant6(p, dst, src, rs, zp, sc, tmp):
    ts1(p, "dve", tmp, src, rs, ALU.mult)
    p.raw("dve", lambda E: E.drain())
    ts2(p, "dve", tmp, tmp, CRND, -CRND, ALU.add, ALU.add)
    p.raw("dve", lambda E: E.drain())
    ts1(p, "dve", tmp, tmp, zp, ALU.add)
    p.raw("dve", lambda E: E.drain())
    ts2(p, "dve", tmp, tmp, 0.0, 255.0, ALU.max, ALU.min)
    p.raw("dve", lambda E: E.drain())
    ts1(p, "dve", tmp, tmp, zp, ALU.subtract)
    p.raw("dve", lambda E: E.drain())
    ts1(p, "dve", dst, tmp, sc, ALU.mult)
    p.raw("dve", lambda E: E.drain())


def qparams(p, PRM, mn_ap, mx_ap, o_rs, o_zp, o_sc, wid):
    A = PRM[:, 56:56 + wid]
    ts1(p, "dve", PRM[:, o_sc:o_sc + wid], mx_ap, 0.0, ALU.max)
    p.raw("dve", lambda E: E.drain())
    ts1(p, "dve", A, mn_ap, 0.0, ALU.min)
    p.raw("dve", lambda E: E.drain())
    tt(p, "dve", PRM[:, o_sc:o_sc + wid], PRM[:, o_sc:o_sc + wid], A, ALU.subtract)
    p.raw("dve", lambda E: E.drain())
    ts2(p, "dve", PRM[:, o_sc:o_sc + wid], PRM[:, o_sc:o_sc + wid], 1.0 / 255.0, 1e-8,
        ALU.mult, ALU.max)
    p.raw("dve", lambda E: E.drain())
    recip(p, PRM[:, o_rs:o_rs + wid], PRM[:, o_sc:o_sc + wid])
    p.raw("dve", lambda E: E.drain())
    tt(p, "dve", PRM[:, o_zp:o_zp + wid], A, PRM[:, o_rs:o_rs + wid], ALU.mult)
    p.raw("dve", lambda E: E.drain())
    ts1(p, "dve", PRM[:, o_zp:o_zp + wid], PRM[:, o_zp:o_zp + wid], -1.0, ALU.mult)
    p.raw("dve", lambda E: E.drain())
    ts2(p, "dve", PRM[:, o_zp:o_zp + wid], PRM[:, o_zp:o_zp + wid], CRND, -CRND,
        ALU.add, ALU.add)
    p.raw("dve", lambda E: E.drain())


def l2_arenas(p, big=True):
    ar = {}
    ar["U32"] = p.sbuf("a_u32", [128, 8192], F32)
    ar["U16"] = p.sbuf("a_u16", [128, 8192], BF16)
    ar["SCR"] = p.sbuf("a_scr", [128, 1024], F32)
    ar["SM"] = p.sbuf("a_sm", [128, 1024], F32)
    ar["QT"] = p.sbuf("a_qt", [128, 1024], BF16)
    ar["QT2"] = p.sbuf("a_qt2", [128, 256], BF16)
    ar["PRM"] = p.sbuf("a_prm", [128, 64], F32)
    ar["BC"] = p.sbuf("a_bc", [128, 16], F32)
    ar["cvec"] = p.sbuf("a_cvec", [128, 8], F32)
    ar["ones128"] = p.sbuf("a_ones", [128, 128], BF16)
    ar["ident"] = p.sbuf("a_id", [128, 128], BF16)
    ar["identf"] = p.sbuf("a_idf", [128, 128], F32)
    ar["ps"] = [p.psum(f"a_ps{i}", [128, 512], F32) for i in range(8)]
    return ar


def l2_load_consts(p, ar, w):
    p.dma("sp", ar["cvec"][:], w["cvec"])
    p.dma("sp", ar["ones128"][:], w["ones128"])
    p.dma("sp", ar["ident"][:], w["ident"])
    p.dma("sp", ar["identf"][:], w["identf"])
    p.barrier()


def wquant_transpose(p, ar, src_w, nrow, dst_T):
    U32, QT, QT2, PRM = ar["U32"], ar["QT"], ar["QT2"], ar["PRM"]
    for r in range(nrow // 128):
        p.wait("sp", "m_dve", p.cnt["m_dve"])
        p.wait("sp", "q_sp", p.cnt["q_sp"] - 16 * 0)
        ld = p.dma("sp", U32[:, 0:1024], src_w[r * 128:(r + 1) * 128, :])
        p.wait("dve", "q_sp", ld)
        p.raw("dve", lambda E: E.tensor_reduce(out=PRM[:, 48:49], in_=U32[:, 0:1024],
                                               axis=mybir.AxisListType.X, op=ALU.min))
        p.raw("dve", lambda E: E.drain())
        p.raw("dve", lambda E: E.tensor_reduce(out=PRM[:, 49:50], in_=U32[:, 0:1024],
                                               axis=mybir.AxisListType.X, op=ALU.max))
        p.raw("dve", lambda E: E.drain())
        qparams(p, PRM, PRM[:, 48:49], PRM[:, 49:50], 40, 41, 42, 1)
        p.wait("dve", "q_sp", p.cnt["q_sp"])
        quant6(p, QT[:, 0:1024], U32[:, 0:1024],
               PRM[:, 40:41], PRM[:, 41:42], PRM[:, 42:43], U32[:, 1024:2048])
        dvq = p.step("dve")
        p.wait("sp", "m_dve", dvq)
        for j in range(8):
            par = j % 2
            p.wait("sp", "q_act", p.cnt["q_act"])
            ldt = p.dma("sp", QT2[:, par * 128:(par + 1) * 128],
                        QT[:, j * 128:(j + 1) * 128], transpose=True)
            p.wait("act", "q_sp", ldt)
            st = p.dma("act", dst_T[j * 128:(j + 1) * 128, r * 128:(r + 1) * 128],
                       QT2[:, par * 128:(par + 1) * 128])
        p.wait("sp", "q_act", p.cnt["q_act"])
    p.barrier()


def emit_l2a(p, w, stop_after=None):
    U32, U16, SCR, SM = (ar := l2_arenas(p))["U32"], None, None, None
    U32, U16, SCR, SM, QT, QT2, PRM, BC = (ar["U32"], ar["U16"], ar["SCR"], ar["SM"],
                                           ar["QT"], ar["QT2"], ar["PRM"], ar["BC"])
    cvec, ones128, ident, identf, ps = (ar["cvec"], ar["ones128"], ar["ident"],
                                        ar["identf"], ar["ps"])
    l2_load_consts(p, ar, w)
    xn = p.sbuf("l2_xn", [128, NCH, NT], BF16)
    F1 = p.sbuf("l2_f1", [128, 2, H2], F32)
    F2 = p.sbuf("l2_f2", [128, 2, H2], F32)
    MMX = p.sbuf("l2_mmx", [128, 16], F32)
    RST = p.sbuf("l2_rst", [128, 48], F32)
    lnar = {"F1": F1, "F2": F2, "SM": SM, "SCR": SCR, "QT": QT, "ps": ps, "cvec": cvec}
    x2f = w["x2f"]
    wqT_d = p.dram("l2_wqT", [1024, 3072], BF16)
    xn0_d = p.dram("l2_xn0", [NCH, 128, NT], BF16)

    for wi in range(NWIN):
        emit_ln(p, x2f[:, wi * NT:(wi + 1) * NT], xn, lnar, ones128, f"l2w{wi}")
        for c in range(NCH):
            p.wait_eng("dve", "act", p.cnt["m_act"])
            p.raw("dve", lambda E, c=c: E.tensor_reduce(
                out=PRM[:, 48:49], in_=xn[:, c, :], axis=mybir.AxisListType.X, op=ALU.min))
            p.raw("dve", lambda E: E.drain())
            p.raw("dve", lambda E, c=c: E.tensor_reduce(
                out=PRM[:, 49:50], in_=xn[:, c, :], axis=mybir.AxisListType.X, op=ALU.max))
            p.raw("dve", lambda E: E.drain())
            if wi == 0:
                copy(p, "dve", MMX[:, c:c + 1], PRM[:, 48:49])
                p.raw("dve", lambda E: E.drain())
                copy(p, "dve", MMX[:, 8 + c:9 + c], PRM[:, 49:50])
            else:
                tt(p, "dve", MMX[:, c:c + 1], MMX[:, c:c + 1], PRM[:, 48:49], ALU.min)
                p.raw("dve", lambda E: E.drain())
                tt(p, "dve", MMX[:, 8 + c:9 + c], MMX[:, 8 + c:9 + c], PRM[:, 49:50], ALU.max)
            p.raw("dve", lambda E: E.drain())
            p.step("dve")
        if wi == 0:
            dv = p.cnt["m_dve"]
            p.wait("act", "m_dve", dv)
            for c in range(NCH):
                p.dma("act", xn0_d[c], xn[:, c, :])
                p.dma("act", w["xn2o"][c * 128:(c + 1) * 128, :], xn[:, c, :].bitcast(BF16))
            p.wait("sp", "q_act", p.cnt["q_act"])
            p.wait("dve", "q_act", p.cnt["q_act"])
    p.barrier()
    if stop_after == "a0":
        return

    qparams(p, PRM, MMX[:, 0:8], MMX[:, 8:16], 0, 8, 16, 8)
    p.step("dve")
    p.barrier()

    for c in range(NCH):
        ld = p.dma("sp", U16[:, 0:4096], xn0_d[c])
        p.wait("dve", "q_sp", ld)
        copy(p, "dve", U32[:, 0:4096], U16[:, 0:4096])
        p.raw("dve", lambda E: E.drain())
        quant6(p, xn[:, c, :], U32[:, 0:4096],
               PRM[:, c:c + 1], PRM[:, 8 + c:9 + c], PRM[:, 16 + c:17 + c], U32[:, 4096:8192])
        dv = p.step("dve")
        p.wait("sp", "m_dve", dv)
    p.barrier()
    if stop_after == "a2":
        xob = w["qkvo"].bitcast(BF16) if False else w["qkvo"]
        for c in range(NCH):
            p.dma("act", xob[c * 128:(c + 1) * 128, 0:4096], xn[:, c, :])
        return

    wquant_transpose(p, ar, w["l2_wqkv"], 3072, wqT_d)
    if stop_after == "a3":
        for j in range(8):
            ld = p.dma("sp", U16[:, 0:3072], wqT_d[j * 128:(j + 1) * 128, :])
            p.wait("act", "q_sp", ld)
            p.dma("act", w["qkvo"][j * 128:(j + 1) * 128, 0:3072], U16[:, 0:3072])
            p.wait("sp", "q_act", p.cnt["q_act"])
        return

    p.raw("dve", lambda E: E.memset(RST[:, 0:24], 1e30))
    p.raw("dve", lambda E: E.memset(RST[:, 24:48], -1e30))
    p.step("dve")
    lb = p.dma("sp", BC[:, 0:16], w["l2_bqkvc"][:, 0:16])
    lb = p.dma("sp", BC[:, 16:24] if False else PRM[:, 50:58], w["l2_bqkvc"][:, 16:24])
    p.wait("act", "q_sp", lb)
    bqcol = lambda oc: BC[:, oc:oc + 1] if oc < 16 else PRM[:, 50 + oc - 16:51 + oc - 16]
    wblk = U16[:, 0:1024].rearrange("p (i n) -> p i n", n=128)
    for oc in range(24):
        p.wait("sp", "m_pe", p.cnt["m_pe"])
        lw = p.dma("sp", U16[:, 0:1024],
                   wqT_d[:, oc * 128:(oc + 1) * 128].rearrange("(i p) n -> p i n", p=128))
        p.wait("pe", "q_sp", lw)
        for tt8 in range(NTT):
            ts = slice(tt8 * 512, (tt8 + 1) * 512)
            pa = ps[tt8 % 2]
            p.wait_eng("pe", "act", p.cnt["m_act"])
            for i in range(NCH):
                mm(p, pa[:], wblk[:, i, :], xn[:, i, ts], start=(i == 0), stop=(i == NCH - 1))
            pec = p.step("pe")
            p.wait_eng("act", "pe", pec)
            p.wait_eng("act", "dve", p.cnt["m_dve"])
            act(p, SCR[:, 0:512], pa[:], AF.Identity, bias=bqcol(oc))
            avc = p.step("act")
            p.wait_eng("dve", "act", avc)
            p.wait("dve", "q_act", p.cnt["q_act"])
            copy(p, "dve", U16[:, 1024:1536], SCR[:, 0:512])
            p.raw("dve", lambda E: E.drain())
            p.raw("dve", lambda E: E.tensor_reduce(out=PRM[:, 48:49], in_=SCR[:, 0:512],
                                                   axis=mybir.AxisListType.X, op=ALU.min))
            p.raw("dve", lambda E: E.drain())
            tt(p, "dve", RST[:, oc:oc + 1], RST[:, oc:oc + 1], PRM[:, 48:49], ALU.min)
            p.raw("dve", lambda E: E.drain())
            p.raw("dve", lambda E: E.tensor_reduce(out=PRM[:, 49:50], in_=SCR[:, 0:512],
                                                   axis=mybir.AxisListType.X, op=ALU.max))
            p.raw("dve", lambda E: E.drain())
            tt(p, "dve", RST[:, 24 + oc:25 + oc], RST[:, 24 + oc:25 + oc], PRM[:, 49:50], ALU.max)
            dvc = p.step("dve")
            p.wait("act", "m_dve", dvc)
            p.dma("act", w["qkvo"][oc * 128:(oc + 1) * 128, ts], U16[:, 1024:1536])
            p.wait("sp", "q_act", p.cnt["q_act"])
    p.barrier()

    ts1(p, "dve", RST[:, 0:24], RST[:, 0:24], -1.0, ALU.mult)
    dvc = p.step("dve")
    p.wait_eng("pe", "dve", dvc)
    pstf = ps[6][:, 0:128]
    p.raw("pe", lambda E: E.transpose(out=pstf[0:48, :], in_=RST[:, 0:48], identity=identf[:]))
    pec = p.step("pe")
    p.wait_eng("act", "pe", pec)
    act(p, SCR[0:48, 0:128], pstf[0:48, :], AF.Copy, scale=1.0, bias=0.0)
    avc = p.step("act")
    p.wait_eng("dve", "act", avc)
    p.raw("dve", lambda E: E.tensor_reduce(out=SM[0:48, 0:1], in_=SCR[0:48, 0:128],
                                           axis=mybir.AxisListType.X, op=ALU.max))
    dvc = p.step("dve")
    p.wait("act", "m_dve", dvc)
    p.dma("act", w["statA"][0:1, 0:48], SM[0:48, 0:1])
    p.wait("sp", "q_act", p.cnt["q_act"])
    p.barrier()


def emit_l2b(p, w, stop_after=None):
    ar = l2_arenas(p)
    U32, U16, SCR, SM, QT, QT2, PRM, BC = (ar["U32"], ar["U16"], ar["SCR"], ar["SM"],
                                           ar["QT"], ar["QT2"], ar["PRM"], ar["BC"])
    cvec, ones128, ident, identf, ps = (ar["cvec"], ar["ones128"], ar["ident"],
                                        ar["identf"], ar["ps"])
    l2_load_consts(p, ar, w)
    RB = p.sbuf("l2_rb2", [128, 4], F32)
    VB = p.sbuf("l2_vb", [128, 16, 72], BF16)
    KQ = p.sbuf("l2_kq", [128, 2, 2048], BF16)
    PT = p.sbuf("l2_pt", [128, 2048], BF16)
    OB = p.sbuf("l2_ob", [128, 4096], F32)
    qq_d = p.dram("l2_qq", [24, 128, NT], BF16)
    vtm_d = p.dram("l2_vtm", [NT, 1024], BF16)
    den_d = p.dram("l2_den", [2, 2048], F32)
    stat_d = p.dram("l2_stat", [1, 64], F32)

    ld = p.dma("sp", SM[0:48, 0:8], w["statAg"])
    p.wait("dve", "q_sp", ld)
    p.raw("dve", lambda E: E.tensor_reduce(out=SM[0:48, 8:9], in_=SM[0:48, 0:8],
                                           axis=mybir.AxisListType.X, op=ALU.max))
    dvc = p.step("dve")
    p.wait("act", "m_dve", dvc)
    st = p.dma("act", stat_d[0:1, 0:48].rearrange("a (b c) -> (a b) c", c=1) if False
               else stat_d[0, 0:48], SM[0:48, 8:9])
    p.wait("sp", "q_act", st)
    ld = p.dma("sp", SCR[0:1, 0:48], stat_d[0, 0:48])
    p.wait("dve", "q_sp", ld)
    p.raw("dve", lambda E: E.tensor_reduce(
        out=SCR[0:1, 48:54], in_=SCR[0:1, 0:48].rearrange("p (a b) -> p a b", b=8),
        axis=mybir.AxisListType.X, op=ALU.max))
    p.raw("dve", lambda E: E.drain())
    ts1(p, "dve", SCR[0:1, 54:57], SCR[0:1, 48:51], -1.0, ALU.mult)
    p.raw("dve", lambda E: E.drain())
    qparams(p, SCR[0:1, :], SCR[0:1, 54:57], SCR[0:1, 51:54], 32, 35, 38, 3)
    dvc = p.step("dve")
    p.wait("act", "m_dve", dvc)
    st = p.dma("act", stat_d[0, 0:9], SCR[0:1, 32:41])
    p.wait("sp", "q_act", st)
    ld = p.dma("sp", BC[:, 0:9], stat_d[0, 0:9].partition_broadcast(128))
    p.wait("dve", "q_sp", ld)
    p.barrier()
    if stop_after == "b0":
        p.dma("act", w["aoo"].bitcast(F32)[0:128, 0:8], BC[:, 0:9][:, 0:8])
        return

    for oc in range(24):
        t = oc // 8
        ld = p.dma("sp", U16[:, 0:4096], w["qkvo"][oc * 128:(oc + 1) * 128, :])
        p.wait("dve", "q_sp", ld)
        copy(p, "dve", U32[:, 0:4096], U16[:, 0:4096])
        p.raw("dve", lambda E: E.drain())
        quant6(p, U16[:, 4096:8192], U32[:, 0:4096],
               BC[:, t:t + 1], BC[:, 3 + t:4 + t], BC[:, 6 + t:7 + t], U32[:, 4096:8192])
        dvc = p.step("dve")
        p.wait("act", "m_dve", dvc)
        p.dma("act", qq_d[oc], U16[:, 4096:8192])
        p.wait("sp", "q_act", p.cnt["q_act"])
        if oc >= 16:
            for j in range(32):
                par = j % 2
                p.wait("sp", "q_act", p.cnt["q_act"])
                ldt = p.dma("sp", QT2[:, par * 128:(par + 1) * 128],
                            U16[:, 4096 + j * 128:4096 + (j + 1) * 128], transpose=True)
                p.wait("act", "q_sp", ldt)
                p.dma("act", vtm_d[j * 128:(j + 1) * 128, (oc - 16) * 128:(oc - 15) * 128],
                      QT2[:, par * 128:(par + 1) * 128])
            p.wait("sp", "q_act", p.cnt["q_act"])
            p.wait("dve", "q_act", p.cnt["q_act"])
    p.barrier()

    p.raw("dve", lambda E: E.memset(VB[:, :, 64:65], 1.0))
    p.step("dve")
    p.raw("dve", lambda E: E.memset(RB[0:64, 0:1], 1e30))
    p.raw("dve", lambda E: E.drain())
    p.raw("dve", lambda E: E.memset(RB[0:64, 1:2], -1e30))
    p.step("dve")
    for b in range(2):
        tb = b * 2048
        for h in range(16):
            ock, ocq, r0 = 8 + h // 2, h // 2, (h % 2) * 64
            p.wait("sp", "m_pe", p.cnt["m_pe"])
            p.wait("sp", "m_dve", p.cnt["m_dve"])
            l1_ = p.dma("sp", KQ[0:64, 0, :], qq_d[ock][r0:r0 + 64, tb:tb + 2048])
            l2_ = p.dma("sp", KQ[0:64, 1, :], qq_d[ocq][r0:r0 + 64, tb:tb + 2048])
            for kc in range(16):
                p.dma("sp", VB[:, kc, 0:64],
                      vtm_d[tb + kc * 128:tb + (kc + 1) * 128, h * 64:(h + 1) * 64])
            lv = p.cnt["q_sp"]
            p.wait("pe", "q_sp", lv)
            for kc in range(16):
                for sb in range(4):
                    p.wait_eng("pe", "act", p.cnt["m_act"])
                    mm(p, ps[sb][:], KQ[0:64, 0, kc * 128:(kc + 1) * 128],
                       KQ[0:64, 1, sb * 512:(sb + 1) * 512], start=True, stop=True)
                    pec = p.step("pe")
                    p.wait_eng("act", "pe", pec)
                    act(p, PT[:, sb * 512:(sb + 1) * 512], ps[sb][:], AF.Exp, scale=0.125)
                    avc = p.step("act")
                    p.wait_eng("pe", "act", avc)
                    mm(p, ps[4 + sb][0:65, :], VB[:, kc, 0:65], PT[:, sb * 512:(sb + 1) * 512],
                       start=(kc == 0), stop=(kc == 15))
                    p.step("pe")
            pec = p.cnt["m_pe"]
            p.wait_eng("act", "pe", pec)
            p.wait_eng("act", "dve", p.cnt["m_dve"])
            for sb in range(4):
                act(p, OB[0:65, sb * 512:(sb + 1) * 512], ps[4 + sb][0:65, :],
                    AF.Copy, scale=1.0, bias=0.0)
                p.raw("act", lambda E: E.drain())
            avc = p.step("act")
            p.wait_eng("dve", "act", avc)
            recip(p, OB[64:65, 2048:4096], OB[64:65, 0:2048])
            dvc = p.step("dve")
            p.wait("act", "m_dve", dvc)
            par2 = (b * 16 + h) % 2
            st = p.dma("act", den_d[par2, :], OB[64:65, 2048:4096])
            p.wait("sp", "q_act", st)
            ld = p.dma("sp", U32[0:64, par2 * 2048:(par2 + 1) * 2048],
                       den_d[par2, :].partition_broadcast(64))
            p.wait("dve", "q_sp", ld)
            tt(p, "dve", OB[0:64, 0:2048], OB[0:64, 0:2048],
               U32[0:64, par2 * 2048:(par2 + 1) * 2048], ALU.mult)
            p.raw("dve", lambda E: E.drain())
            copy(p, "dve", PT[0:64, 0:2048], OB[0:64, 0:2048])
            p.raw("dve", lambda E: E.drain())
            p.raw("dve", lambda E: E.tensor_reduce(out=PRM[0:64, 48:49], in_=OB[0:64, 0:2048],
                                                   axis=mybir.AxisListType.X, op=ALU.min))
            p.raw("dve", lambda E: E.drain())
            tt(p, "dve", RB[0:64, 0:1], RB[0:64, 0:1], PRM[0:64, 48:49], ALU.min)
            p.raw("dve", lambda E: E.drain())
            p.raw("dve", lambda E: E.tensor_reduce(out=PRM[0:64, 49:50], in_=OB[0:64, 0:2048],
                                                   axis=mybir.AxisListType.X, op=ALU.max))
            p.raw("dve", lambda E: E.drain())
            tt(p, "dve", RB[0:64, 1:2], RB[0:64, 1:2], PRM[0:64, 49:50], ALU.max)
            dvc = p.step("dve")
            p.wait("act", "m_dve", dvc)
            p.dma("act", w["aoo"][h * 64:(h + 1) * 64, tb:tb + 2048], PT[0:64, 0:2048])
            p.wait("sp", "q_act", p.cnt["q_act"])
    p.barrier()

    ts1(p, "dve", RB[0:64, 0:1], RB[0:64, 0:1], -1.0, ALU.mult)
    dvc = p.step("dve")
    p.wait_eng("pe", "dve", dvc)
    pstf = ps[6][:, 0:128]
    p.raw("pe", lambda E: E.transpose(out=pstf[0:2, 0:64], in_=RB[0:64, 0:2],
                                      identity=identf[0:64, 0:64]))
    pec = p.step("pe")
    p.wait_eng("act", "pe", pec)
    act(p, SCR[0:2, 0:64], pstf[0:2, 0:64], AF.Copy, scale=1.0, bias=0.0)
    avc = p.step("act")
    p.wait_eng("dve", "act", avc)
    p.raw("dve", lambda E: E.tensor_reduce(out=SM[0:2, 0:1], in_=SCR[0:2, 0:64],
                                           axis=mybir.AxisListType.X, op=ALU.max))
    dvc = p.step("dve")
    p.wait("act", "m_dve", dvc)
    p.dma("act", w["statB"][0:1, 0:2], SM[0:2, 0:1])
    p.wait("sp", "q_act", p.cnt["q_act"])
    p.barrier()


def emit_l2c(p, w, stop_after=None):
    ar = l2_arenas(p)
    U32, U16, SCR, SM, QT, QT2, PRM, BC = (ar["U32"], ar["U16"], ar["SCR"], ar["SM"],
                                           ar["QT"], ar["QT2"], ar["PRM"], ar["BC"])
    cvec, ones128, ident, identf, ps = (ar["cvec"], ar["ones128"], ar["ident"],
                                        ar["identf"], ar["ps"])
    l2_load_consts(p, ar, w)
    OQA = p.sbuf("l2_oqa", [128, NCH, NT], BF16)
    BASE = p.sbuf("l2_base", [128, NCH, 1024], BF16)
    PW = p.sbuf("l2_pw", [128, NCH, 1024], BF16)
    RB6 = p.sbuf("l2_rb6", [128, 6, 1024], BF16)
    SAB = p.sbuf("l2_sab", [128, NCH, 16], BF16)
    SBB = p.sbuf("l2_sbb", [128, 1024], BF16)
    RWT = p.sbuf("l2_rwt", [128, NCH, 8], BF16)
    XNT = p.sbuf("l2_xnt", [128, NCH, 512], BF16)
    TH = p.sbuf("l2_th", [128, 8, 512], BF16)
    HM = p.sbuf("l2_hm", [128, 6, 512], BF16)
    RWB = p.sbuf("l2_rwb", [128, 12, 512], F32)
    LG = p.sbuf("l2_lg", [128, 128], F32)
    BCOL = p.sbuf("l2_bcol", [128, 16], F32)
    pwT_d = p.dram("l2_pwT", [1024, 1024], BF16)
    rw_d = p.dram("l2_rw", [2, 8, 512], F32)
    stat_d = p.dram("l2c_stat", [1, 16], F32)

    ld = p.dma("sp", SM[0:2, 0:8], w["statBg"])
    p.wait("dve", "q_sp", ld)
    p.raw("dve", lambda E: E.tensor_reduce(out=SM[0:2, 8:9], in_=SM[0:2, 0:8],
                                           axis=mybir.AxisListType.X, op=ALU.max))
    dvc = p.step("dve")
    p.wait("act", "m_dve", dvc)
    st0 = p.dma("act", stat_d[0, 0:2], SM[0:2, 8:9])
    p.wait("sp", "q_act", st0)
    ld0 = p.dma("sp", SCR[0:1, 0:2], stat_d[0, 0:2])
    p.wait("dve", "q_sp", ld0)
    ts1(p, "dve", SM[0:1, 10:11], SCR[0:1, 0:1], -1.0, ALU.mult)
    p.raw("dve", lambda E: E.drain())
    copy(p, "dve", SM[0:1, 11:12], SCR[0:1, 1:2])
    p.raw("dve", lambda E: E.drain())
    p.raw("dve", lambda E: E.memset(SM[0:1, 32:64], 0.0))
    p.raw("dve", lambda E: E.drain())
    qparams(p, SM[0:1, :], SM[0:1, 10:11], SM[0:1, 11:12], 32, 35, 38, 1)
    dvc = p.step("dve")
    p.wait("act", "m_dve", dvc)
    st = p.dma("act", stat_d[0, 0:9], SM[0:1, 32:41])
    p.wait("sp", "q_act", st)
    ld = p.dma("sp", BC[:, 0:9], stat_d[0, 0:9].partition_broadcast(128))
    p.wait("dve", "q_sp", ld)
    p.barrier()

    for c in range(NCH):
        ld = p.dma("sp", U16[:, 0:4096], w["aoo"][c * 128:(c + 1) * 128, :])
        p.wait("dve", "q_sp", ld)
        copy(p, "dve", U32[:, 0:4096], U16[:, 0:4096])
        p.raw("dve", lambda E: E.drain())
        quant6(p, OQA[:, c, :], U32[:, 0:4096],
               BC[:, 0:1], BC[:, 3:4], BC[:, 6:7], U32[:, 4096:8192])
        dv = p.step("dve")
        p.wait("sp", "m_dve", dv)
    p.barrier()
    if stop_after == "c1":
        xob = w["x3o"].bitcast(BF16)
        for c in range(NCH):
            p.dma("act", xob[c * 128:(c + 1) * 128, 0:4096], OQA[:, c, :])
        return

    wquant_transpose(p, ar, w["l2_wproj"], 1024, pwT_d)

    def ldcast(dst, src_ap, ncol):
        p.wait("sp", "m_dve", p.cnt["m_dve"])
        ldx = p.dma("sp", SCR[:, 0:ncol], src_ap)
        p.wait("dve", "q_sp", ldx)
        p.wait_eng("dve", "pe", p.cnt["m_pe"])
        copy(p, "dve", dst, SCR[:, 0:ncol])
        p.raw("dve", lambda E: E.drain())
        return p.step("dve")

    for i in range(NCH):
        ld = p.dma("sp", PW[:, i, :], pwT_d[i * 128:(i + 1) * 128, :])
        ldcast(BASE[:, i, :], w["l2_baseT"][i * 128:(i + 1) * 128, :], 1024)
        ldcast(SAB[:, i, :], w["l2_sAT"][i * 128:(i + 1) * 128, :], 16)
        ldcast(RWT[:, i, :], w["l2_rwT"][i * 128:(i + 1) * 128, :], 8)
    p.wait("dve", "q_sp", p.cnt["q_sp"])
    for e in range(6):
        p.wait("sp", "m_dve", p.cnt["m_dve"])
        le = p.dma("sp", SM[0:16, 0:1024], w["l2_rBT6"][e])
        p.wait("dve", "q_sp", le)
        copy(p, "dve", RB6[0:16, e, :], SM[0:16, 0:1024])
        p.raw("dve", lambda E: E.drain())
        p.step("dve")
    p.wait("sp", "m_dve", p.cnt["m_dve"])
    ls = p.dma("sp", SM[0:16, 0:1024], w["l2_sBT"])
    p.wait("dve", "q_sp", ls)
    copy(p, "dve", SBB[0:16, :], SM[0:16, 0:1024])
    p.step("dve")
    p.wait("sp", "m_dve", p.cnt["m_dve"])
    lb = p.dma("sp", BCOL[:, 0:8], w["l2_bprojc"])
    lb = p.dma("sp", BCOL[:, 8:16], w["l2_bbasec"])
    p.wait("dve", "q_sp", lb)
    tt(p, "dve", BCOL[:, 0:8], BCOL[:, 0:8], BCOL[:, 8:16], ALU.add)
    p.step("dve")
    lrb = p.dma("sp", PRM[0:8, 0:1], w["l2_rbcol"])
    p.wait("act", "q_sp", lrb)
    p.wait("dve", "q_sp", lrb)
    p.barrier()

    for tt8 in range(NTT):
        ts = slice(tt8 * 512, (tt8 + 1) * 512)
        p.wait("sp", "m_pe", p.cnt["m_pe"])
        for c in range(NCH):
            p.dma("sp", XNT[:, c, :], w["xn2o"][c * 128:(c + 1) * 128, ts])
        lxn = p.cnt["q_sp"]
        p.wait("pe", "q_sp", lxn)
        p.wait_eng("pe", "act", p.cnt["m_act"])
        p.wait_eng("pe", "dve", p.cnt["m_dve"])
        for e in range(6):
            for i in range(NCH):
                mm(p, ps[1][0:16, :], rAb_slice(SAB, w, p, U16, e, i) if False else ps[1][0:16, :],
                   XNT[:, i, :], start=(i == 0), stop=(i == NCH - 1)) if False else None
        for i in range(NCH):
            mm(p, ps[0][0:16, :], SAB[:, i, :], XNT[:, i, :], start=(i == 0), stop=(i == NCH - 1))
        pec = p.step("pe")
        p.wait_eng("act", "pe", pec)
        act(p, TH[0:16, 0, :], ps[0][0:16, :], AF.Copy, scale=1.0, bias=0.0)
        avc = p.step("act")
        for e in range(6):
            p.wait("sp", "m_dve", p.cnt["m_dve"])
            lra = p.dma("sp", SCR[:, 0:128].rearrange("p (i n) -> p i n", n=16),
                        w["l2_rAT6"][e].rearrange("(i p) n -> p i n", p=128))
            p.wait("dve", "q_sp", lra)
            p.wait_eng("dve", "pe", p.cnt["m_pe"])
            copy(p, "dve", U16[:, 0:128], SCR[:, 0:128])
            p.raw("dve", lambda E: E.drain())
            dvc = p.step("dve")
            p.wait_eng("pe", "dve", dvc)
            p.wait_eng("pe", "act", p.cnt["m_act"])
            rab = U16[:, 0:128].rearrange("p (i n) -> p i n", n=16)
            for i in range(NCH):
                mm(p, ps[1][0:16, :], rab[:, i, :], XNT[:, i, :], start=(i == 0), stop=(i == NCH - 1))
            pec = p.step("pe")
            p.wait_eng("act", "pe", pec)
            act(p, TH[0:16, 1 + e, :], ps[1][0:16, :], AF.Copy, scale=1.0, bias=0.0)
            p.raw("act", lambda E: E.drain())
            avc = p.step("act")
        p.wait_eng("pe", "act", avc)
        for i in range(NCH):
            mm(p, ps[2][0:8, :], RWT[:, i, :], XNT[:, i, :], start=(i == 0), stop=(i == NCH - 1))
        pec = p.step("pe")
        p.wait_eng("act", "pe", pec)
        act(p, U32[0:8, 0:512], ps[2][0:8, :], AF.Identity, bias=PRM[0:8, 0:1])
        avc = p.step("act")
        p.wait_eng("pe", "act", avc)
        for j4 in range(4):
            p.raw("pe", lambda E, j4=j4: E.transpose(
                out=ps[6][:, 0:128][0:128, 0:8], in_=U32[0:8, j4 * 128:(j4 + 1) * 128],
                identity=identf[0:8, 0:8]))
            pec = p.step("pe")
            p.wait_eng("act", "pe", pec)
            act(p, LG[:, j4 * 8:(j4 + 1) * 8], ps[6][:, 0:128][0:128, 0:8], AF.Copy,
                scale=1.0, bias=0.0)
            avc = p.step("act")
            p.wait_eng("pe", "act", avc)
        LG3 = LG[:, 0:32].rearrange("p (a b) -> p a b", b=8)
        p.wait_eng("dve", "act", p.cnt["m_act"])
        p.raw("dve", lambda E: E.tensor_reduce(out=LG[:, 32:36], in_=LG3,
                                               axis=mybir.AxisListType.X, op=ALU.max))
        p.raw("dve", lambda E: E.drain())
        tt(p, "dve", LG3, LG3, rep_in(LG[:, 32:36], 8), ALU.subtract)
        dvc = p.step("dve")
        p.wait_eng("act", "dve", dvc)
        act(p, LG[:, 0:32], LG[:, 0:32], AF.Exp)
        avc = p.step("act")
        p.wait_eng("dve", "act", avc)
        p.raw("dve", lambda E: E.tensor_reduce(out=LG[:, 32:36], in_=LG3,
                                               axis=mybir.AxisListType.X, op=ALU.add))
        p.raw("dve", lambda E: E.drain())
        recip(p, LG[:, 36:40], LG[:, 32:36])
        p.raw("dve", lambda E: E.drain())
        tt(p, "dve", LG3, LG3, rep_in(LG[:, 36:40], 8), ALU.mult)
        p.raw("dve", lambda E: E.drain())
        p.raw("dve", lambda E: E.tensor_reduce(out=LG[:, 32:36], in_=LG3,
                                               axis=mybir.AxisListType.X, op=ALU.max))
        p.raw("dve", lambda E: E.drain())
        MK = LG[:, 40:72].rearrange("p (a b) -> p a b", b=8)
        tt(p, "dve", MK, LG3, rep_in(LG[:, 32:36], 8), ALU.is_ge)
        p.raw("dve", lambda E: E.drain())
        G2 = U32[:, 0:32].rearrange("p (a b) -> p a b", b=8)
        tt(p, "dve", G2, LG3, MK, ALU.mult)
        p.raw("dve", lambda E: E.drain())
        tt(p, "dve", G2, LG3, G2, ALU.subtract)
        p.raw("dve", lambda E: E.drain())
        p.raw("dve", lambda E: E.tensor_reduce(out=LG[:, 32:36], in_=G2,
                                               axis=mybir.AxisListType.X, op=ALU.max))
        p.raw("dve", lambda E: E.drain())
        tt(p, "dve", G2, G2, rep_in(LG[:, 32:36], 8), ALU.is_ge)
        p.raw("dve", lambda E: E.drain())
        tt(p, "dve", MK, MK, G2, ALU.add)
        p.raw("dve", lambda E: E.drain())
        tt(p, "dve", MK, MK, LG3, ALU.mult)
        p.raw("dve", lambda E: E.drain())
        p.raw("dve", lambda E: E.tensor_reduce(out=LG[:, 32:36], in_=MK,
                                               axis=mybir.AxisListType.X, op=ALU.add))
        p.raw("dve", lambda E: E.drain())
        ts1(p, "dve", LG[:, 32:36], LG[:, 32:36], 1e-6, ALU.add)
        p.raw("dve", lambda E: E.drain())
        recip(p, LG[:, 36:40], LG[:, 32:36])
        p.raw("dve", lambda E: E.drain())
        tt(p, "dve", MK, MK, rep_in(LG[:, 36:40], 8), ALU.mult)
        dvc = p.step("dve")
        p.wait_eng("pe", "dve", dvc)
        for j4 in range(4):
            p.raw("pe", lambda E, j4=j4: E.transpose(
                out=ps[6][:, 0:128][0:8, 0:128], in_=LG[:, 40 + j4 * 8:48 + j4 * 8],
                identity=identf[:]))
            pec = p.step("pe")
            p.wait_eng("act", "pe", pec)
            p.wait("act", "q_act", p.cnt["q_act"])
            act(p, SM[0:8, j4 * 128:(j4 + 1) * 128], ps[6][:, 0:128][0:8, 0:128],
                AF.Copy, scale=1.0, bias=0.0)
            p.raw("act", lambda E: E.drain())
            avc = p.step("act")
            p.wait_eng("pe", "act", avc)
        par2 = tt8 % 2
        st = p.dma("act", rw_d[par2], SM[0:8, 0:512])
        p.wait("sp", "q_act", st)
        for e in range(6):
            p.dma("sp", RWB[0:16, par2 * 6 + e, :], rw_d[par2, e, :].partition_broadcast(16))
        lbc = p.cnt["q_sp"]
        p.wait("dve", "q_sp", lbc)
        p.wait_eng("dve", "act", p.cnt["m_act"])
        for e in range(6):
            tt(p, "dve", HM[0:16, e, :], TH[0:16, 1 + e, :], RWB[0:16, par2 * 6 + e, :], ALU.mult)
            p.raw("dve", lambda E: E.drain())
        dvhm = p.step("dve")
        p.wait_eng("pe", "dve", dvhm)
        for oc in range(NCH):
            pa = ps[4 + (oc % 2)]
            p.wait_eng("pe", "act", p.cnt["m_act"])
            for i in range(NCH):
                mm(p, pa[:], PW[:, i, oc * 128:(oc + 1) * 128], OQA[:, i, ts],
                   start=(i == 0), stop=False)
            for i in range(NCH):
                mm(p, pa[:], BASE[:, i, oc * 128:(oc + 1) * 128], XNT[:, i, :],
                   start=False, stop=False)
            mm(p, pa[:], SBB[0:16, oc * 128:(oc + 1) * 128], TH[0:16, 0, :],
               start=False, stop=False)
            for e in range(6):
                mm(p, pa[:], RB6[0:16, e, oc * 128:(oc + 1) * 128], HM[0:16, e, :],
                   start=False, stop=(e == 5))
            pec = p.step("pe")
            p.wait_eng("act", "pe", pec)
            p.wait_eng("act", "dve", p.cnt["m_dve"])
            act(p, SCR[:, 0:512], pa[:], AF.Identity, bias=BCOL[:, oc:oc + 1])
            avc = p.step("act")
            p.wait_eng("dve", "act", avc)
            lx = p.dma("sp", U32[:, 2048:2560], w["x2o"][oc * 128:(oc + 1) * 128, ts])
            p.wait("dve", "q_sp", lx)
            tt(p, "dve", U32[:, 2560:3072], SCR[:, 0:512], U32[:, 2048:2560], ALU.add)
            dvc = p.step("dve")
            p.wait("act", "m_dve", dvc)
            p.dma("act", w["x3o"][oc * 128:(oc + 1) * 128, ts], U32[:, 2560:3072])
            p.wait("sp", "q_act", p.cnt["q_act"])
            p.wait("dve", "q_act", p.cnt["q_act"])
    p.barrier()


def l2_host_inputs(inp):
    d = {}
    d["l2_wqkv"] = np.asarray(inp["L2_attn_qkv_w"], np.float32)
    d["l2_bqkvc"] = col_pack(inp["L2_attn_qkv_b"], 24)
    d["l2_wproj"] = np.asarray(inp["L2_attn_proj_w"], np.float32)
    d["l2_bprojc"] = col_pack(inp["L2_attn_proj_b"], 8)
    d["l2_bbasec"] = col_pack(inp["L2_ffn_base_b"], 8)
    d["l2_baseT"] = np.ascontiguousarray(np.asarray(inp["L2_ffn_base_W"], np.float32).T)
    d["l2_sAT"] = np.ascontiguousarray(np.asarray(inp["L2_ffn_shared_A"], np.float32).T)
    d["l2_sBT"] = np.ascontiguousarray(np.asarray(inp["L2_ffn_shared_B"], np.float32).T)
    d["l2_rAT6"] = np.ascontiguousarray(np.transpose(np.asarray(inp["L2_ffn_routed_A"], np.float32), (0, 2, 1)))
    d["l2_rBT6"] = np.ascontiguousarray(np.transpose(np.asarray(inp["L2_ffn_routed_B"], np.float32), (0, 2, 1)))
    rw = np.zeros((1024, 8), np.float32)
    rw[:, 0:6] = np.asarray(inp["L2_ffn_router_W"], np.float32).T
    d["l2_rwT"] = rw
    rb = np.full((8, 1), -1e30, np.float32)
    rb[0:6, 0] = np.asarray(inp["L2_ffn_router_b"], np.float32)
    d["l2_rbcol"] = rb
    return d


def build_launch_l2a(stop_after=None):
    p = Prog("l2a")
    w = {}
    w["x2f"] = p.inp("x2f", [D, NTG], F32)
    w["l2_wqkv"] = p.inp("l2_wqkv", [3072, 1024], F32)
    w["l2_bqkvc"] = p.inp("l2_bqkvc", [128, 24], F32)
    for name, shape, dt in [("cvec", [128, 8], F32), ("ones128", [128, 128], BF16),
                            ("ident", [128, 128], BF16), ("identf", [128, 128], F32)]:
        w[name] = p.inp(name, shape, dt)
    w["xn2o"] = p.outp("xn2o", [D, NT], BF16)
    w["qkvo"] = p.outp("qkvo", [3072, NT], BF16)
    w["statA"] = p.outp("statA", [1, 48], F32)
    emit_l2a_cin(p, w, stop_after)
    p.finalize()
    return p


def emit_l2a_cin(p, w, stop_after):
    emit_l2a(p, w, stop_after)


def build_launch_l2b(stop_after=None):
    p = Prog("l2b")
    w = {}
    w["qkvo"] = p.inp("qkvo", [3072, NT], BF16)
    w["statAg"] = p.inp("statAg", [48, 8], F32)
    for name, shape, dt in [("cvec", [128, 8], F32), ("ones128", [128, 128], BF16),
                            ("ident", [128, 128], BF16), ("identf", [128, 128], F32)]:
        w[name] = p.inp(name, shape, dt)
    w["aoo"] = p.outp("aoo", [D, NT], BF16)
    w["statB"] = p.outp("statB", [1, 2], F32)
    emit_l2b(p, w, stop_after)
    p.finalize()
    return p


def build_launch_l2c(stop_after=None):
    p = Prog("l2c")
    w = {}
    w["x2o"] = p.inp("x2o", [D, NT], F32)
    w["xn2o"] = p.inp("xn2o", [D, NT], BF16)
    w["aoo"] = p.inp("aoo", [D, NT], BF16)
    w["statBg"] = p.inp("statBg", [2, 8], F32)
    for name, shape, dt in [
        ("l2_wproj", [1024, 1024], F32), ("l2_bprojc", [128, 8], F32),
        ("l2_bbasec", [128, 8], F32), ("l2_baseT", [1024, 1024], F32),
        ("l2_sAT", [1024, 16], F32), ("l2_sBT", [16, 1024], F32),
        ("l2_rAT6", [6, 1024, 16], F32), ("l2_rBT6", [6, 16, 1024], F32),
        ("l2_rwT", [1024, 8], F32), ("l2_rbcol", [8, 1], F32),
        ("cvec", [128, 8], F32), ("ones128", [128, 128], BF16),
        ("ident", [128, 128], BF16), ("identf", [128, 128], F32),
    ]:
        w[name] = p.inp(name, shape, dt)
    w["x3o"] = p.outp("x3o", [D, NT], F32)
    emit_l2c(p, w, stop_after)
    p.finalize()
    return p



FP4_MID = (0.25, 0.75, 1.25, 1.75, 2.5, 3.5, 5.0)
FP4_STEP = (0.5, 0.5, 0.5, 0.5, 1.0, 1.0, 2.0)
L3_SC = 1.0 / (64.0 ** 0.25)


def fp4q(p, dst, src, W, T1, T2, AB, ACC, ACC2, BSC):
    nb = W // 16
    ts1(p, "dve", T1, src, -1.0, ALU.mult)
    tt(p, "dve", AB, src, T1, ALU.max)
    p.raw("dve", lambda E: E.drain())
    p.raw("dve", lambda E: E.tensor_reduce(
        out=BSC[:, 0:nb], in_=AB.rearrange("p (a b) -> p a b", b=16),
        axis=mybir.AxisListType.X, op=ALU.max))
    p.raw("dve", lambda E: E.drain())
    ts2(p, "dve", BSC[:, nb:2 * nb], BSC[:, 0:nb], 1.0 / 6.0, 1e-12, ALU.mult, ALU.max)
    p.raw("dve", lambda E: E.drain())
    recip(p, BSC[:, 2 * nb:3 * nb], BSC[:, nb:2 * nb])
    p.raw("dve", lambda E: E.drain())
    tt(p, "dve", T2, src, rep_in(BSC[:, 2 * nb:3 * nb], 16), ALU.mult)
    tt(p, "dve", AB, AB, rep_in(BSC[:, 2 * nb:3 * nb], 16), ALU.mult)
    p.raw("dve", lambda E: E.memset(ACC, 0.0))
    a, b2 = ACC, ACC2
    for mid, st in zip(FP4_MID, FP4_STEP):
        ts1(p, "dve", T1, AB, mid, ALU.is_gt)
        stt(p, "dve", b2, T1, st, a, ALU.mult, ALU.add)
        a, b2 = b2, a
    ts1(p, "dve", T1, T2, 0.0, ALU.is_ge)
    ts2(p, "dve", T1, T1, 2.0, -1.0, ALU.mult, ALU.add)
    tt(p, "dve", a, a, T1, ALU.mult)
    tt(p, "dve", dst, a, rep_in(BSC[:, nb:2 * nb], 16), ALU.mult)
    p.raw("dve", lambda E: E.drain())


def emit_l3(p, w, stop_after=None):
    ar = l2_arenas(p)
    U32, U16, SCR, SM, QT, QT2, PRM, BC = (ar["U32"], ar["U16"], ar["SCR"], ar["SM"],
                                           ar["QT"], ar["QT2"], ar["PRM"], ar["BC"])
    cvec, ones128, ident, identf, ps = (ar["cvec"], ar["ones128"], ar["ident"],
                                        ar["identf"], ar["ps"])
    l2_load_consts(p, ar, w)
    xn = p.sbuf("l3_xn", [128, NCH, NT], BF16)
    F1 = p.sbuf("l3_f1", [128, 2, H2], F32)
    F2 = p.sbuf("l3_f2", [128, 2, H2], F32)
    WQB = p.sbuf("l3_wqb", [128, NCH, 1024], BF16)
    WKB = p.sbuf("l3_wkb", [128, NCH, 1024], BF16)
    BQC = p.sbuf("l3_bqcs", [128, 24], F32)
    GMX = p.sbuf("l3_gmx", [128, 8], F32)
    lnar = {"F1": F1, "F2": F2, "SM": SM, "SCR": SCR, "QT": QT, "ps": ps, "cvec": cvec}
    x3f = w["x3f"]
    xn3_d = p.dram("l3_xn3", [NCH, 128, NTG], BF16)
    Qd = p.dram("l3_Q", [NCH, 128, NTG], BF16)
    Kd = p.dram("l3_K", [NCH, 128, NTG], BF16)
    qlm_d = p.dram("l3_qlm", [NCH, 128, 2048], BF16)
    klm_d = p.dram("l3_klm", [NCH, 128, 2048], BF16)
    vtm_d = p.dram("l3_vtm", [NT, 1024], BF16)
    k2_d = p.dram("l3_k2", [32, 128, 128], BF16)
    xat_d = p.dram("l3_xat", [NCH, 128, NT], BF16)
    gs_d = p.dram("l3_gs", [1, 8], F32)

    for wi in range(NWIN):
        emit_ln(p, x3f[:, wi * NT:(wi + 1) * NT], xn, lnar, ones128, f"l3w{wi}")
        for c in range(NCH):
            p.dma("act", xn3_d[c][:, wi * NT:(wi + 1) * NT], xn[:, c, :])
        p.wait("sp", "q_act", p.cnt["q_act"])
        p.wait("dve", "q_act", p.cnt["q_act"])
    p.barrier()
    if stop_after == "r0":
        return

    def ldcast(dst, src_ap, ncol):
        p.wait("sp", "m_dve", p.cnt["m_dve"])
        ldx = p.dma("sp", SCR[:, 0:ncol], src_ap)
        p.wait("dve", "q_sp", ldx)
        p.wait_eng("dve", "pe", p.cnt["m_pe"])
        copy(p, "dve", dst, SCR[:, 0:ncol])
        p.raw("dve", lambda E: E.drain())
        return p.step("dve")
    for i in range(NCH):
        ldcast(WQB[:, i, :], w["l3_WqT"][i * 128:(i + 1) * 128, :], 1024)
        ldcast(WKB[:, i, :], w["l3_WkT"][i * 128:(i + 1) * 128, :], 1024)
    lb = p.dma("sp", BQC[:, 0:8], w["l3_bqc"])
    lb = p.dma("sp", BQC[:, 8:16], w["l3_bkc"])
    lb = p.dma("sp", BQC[:, 16:24], w["l3_bvc"])
    p.wait("dve", "q_sp", lb)
    ts1(p, "dve", BQC[:, 0:16], BQC[:, 0:16], L3_SC, ALU.mult)
    p.step("dve")
    p.barrier()
    XNT = U16[:, 0:4096].rearrange("p (i n) -> p i n", n=512)
    for gt in range(NTG // 512):
        p.wait("sp", "m_pe", p.cnt["m_pe"])
        for i in range(NCH):
            p.dma("sp", XNT[:, i, :], xn3_d[i][:, gt * 512:(gt + 1) * 512])
        lx = p.cnt["q_sp"]
        p.wait("pe", "q_sp", lx)
        for pj, (wb, dd, bo) in enumerate(((WQB, Qd, 0), (WKB, Kd, 8))):
            for oc in range(NCH):
                pa = ps[(pj * 8 + oc) % 2]
                p.wait_eng("pe", "act", p.cnt["m_act"])
                for i in range(NCH):
                    mm(p, pa[:], wb[:, i, oc * 128:(oc + 1) * 128], XNT[:, i, :],
                       start=(i == 0), stop=(i == NCH - 1))
                pec = p.step("pe")
                p.wait_eng("act", "pe", pec)
                p.wait("act", "q_act", max(0, p.cnt["q_act"] - 16))
                act(p, QT[:, (oc % 2) * 512:(oc % 2 + 1) * 512], pa[:], AF.Identity,
                    scale=L3_SC, bias=BQC[:, bo + oc:bo + oc + 1])
                p.step("act")
                p.dma("act", dd[oc][:, gt * 512:(gt + 1) * 512],
                      QT[:, (oc % 2) * 512:(oc % 2 + 1) * 512])
        p.wait("sp", "q_act", p.cnt["q_act"])
    p.barrier()
    for i in range(NCH):
        ldcast(WQB[:, i, :], w["l3_WvT"][i * 128:(i + 1) * 128, :], 1024)
    p.barrier()
    for oc in range(NCH):
        for tt8 in range(NTT):
            ts = slice(tt8 * 512, (tt8 + 1) * 512)
            p.wait("sp", "m_pe", p.cnt["m_pe"])
            for i in range(NCH):
                p.dma("sp", XNT[:, i, :], xn3_d[i][:, ts])
            lx = p.cnt["q_sp"]
            p.wait("pe", "q_sp", lx)
            pa = ps[tt8 % 2]
            p.wait_eng("pe", "act", p.cnt["m_act"])
            for i in range(NCH):
                mm(p, pa[:], WQB[:, i, oc * 128:(oc + 1) * 128], XNT[:, i, :],
                   start=(i == 0), stop=(i == NCH - 1))
            pec = p.step("pe")
            p.wait_eng("act", "pe", pec)
            p.wait("act", "q_act", p.cnt["q_act"])
            act(p, QT[:, 0:512], pa[:], AF.Identity, bias=BQC[:, 16 + oc:17 + oc])
            avq = p.step("act")
            p.wait("sp", "m_act", avq)
            for j in range(4):
                par = j % 2
                p.wait("sp", "q_act", p.cnt["q_act"])
                ldt = p.dma("sp", QT2[:, par * 128:(par + 1) * 128],
                            QT[:, j * 128:(j + 1) * 128], transpose=True)
                p.wait("act", "q_sp", ldt)
                p.dma("act", vtm_d[tt8 * 512 + j * 128:tt8 * 512 + (j + 1) * 128,
                                   oc * 128:(oc + 1) * 128],
                      QT2[:, par * 128:(par + 1) * 128])
            p.wait("sp", "q_act", p.cnt["q_act"])
    p.barrier()
    if stop_after == "r1":
        return

    for dd, ld_ in ((Qd, qlm_d), (Kd, klm_d)):
        for oc in range(NCH):
            for wi in range(NWIN):
                ldx = p.dma("sp", U16[:, 0:4096], dd[oc][:, wi * NT:(wi + 1) * NT])
                p.wait("dve", "q_sp", ldx)
                p.raw("dve", lambda E: E.tensor_reduce(
                    out=U32[:, 0:256], in_=U16[:, 0:4096].rearrange("p (a b) -> p a b", b=16),
                    axis=mybir.AxisListType.X, op=ALU.add))
                p.raw("dve", lambda E: E.drain())
                ts1(p, "dve", U16[:, 4096:4352], U32[:, 0:256], 1.0 / 16.0, ALU.mult)
                dvc = p.step("dve")
                p.wait("act", "m_dve", dvc)
                p.dma("act", ld_[oc][:, wi * 256:(wi + 1) * 256], U16[:, 4096:4352])
                p.wait("sp", "q_act", p.cnt["q_act"])
                p.wait("dve", "q_act", p.cnt["q_act"])
    p.barrier()

    p.raw("dve", lambda E: E.memset(GMX[0:1, 0:1], -1e30))
    p.step("dve")
    for b in range(16):
        for h in range(16):
            oc, r0 = h // 2, (h % 2) * 64
            p.wait("sp", "m_pe", p.cnt["m_pe"])
            l1_ = p.dma("sp", U16[0:64, 0:128], qlm_d[oc][r0:r0 + 64, b * 128:(b + 1) * 128])
            l2_ = p.dma("sp", U16[0:64, 128:256], klm_d[oc][r0:r0 + 64, b * 128:(b + 1) * 128])
            p.wait("pe", "q_sp", l2_)
            p.wait_eng("pe", "act", p.cnt["m_act"])
            p.wait_eng("pe", "dve", p.cnt["m_dve"])
            mm(p, ps[0][:, 0:128], U16[0:64, 0:128], U16[0:64, 128:256], start=True, stop=True)
            pec = p.step("pe")
            p.wait_eng("act", "pe", pec)
            act(p, SCR[:, 0:128], ps[0][:, 0:128], AF.Exp)
            avc = p.step("act")
            p.wait_eng("dve", "act", avc)
            p.raw("dve", lambda E: E.tensor_reduce(out=PRM[:, 48:49], in_=SCR[:, 0:128],
                                                   axis=mybir.AxisListType.X, op=ALU.add))
            p.raw("dve", lambda E: E.drain())
            recip(p, PRM[:, 49:50], PRM[:, 48:49])
            p.raw("dve", lambda E: E.drain())
            ts1(p, "dve", U16[:, 256:384], SCR[:, 0:128], PRM[:, 49:50], ALU.mult)
            dvc = p.step("dve")
            if b < 2:
                p.wait("act", "m_dve", dvc)
                p.dma("act", k2_d[b * 16 + h], U16[:, 256:384])
            p.wait_eng("pe", "dve", dvc)
            mm(p, ps[1][0:1, 0:128], ones128[:, 0:1], U16[:, 256:384], start=True, stop=True)
            pec = p.step("pe")
            p.wait_eng("act", "pe", pec)
            act(p, SM[0:1, 0:128], ps[1][0:1, 0:128], AF.Copy, scale=1.0, bias=0.0)
            avc = p.step("act")
            p.wait_eng("dve", "act", avc)
            p.raw("dve", lambda E: E.tensor_reduce(out=PRM[0:1, 50:51], in_=SM[0:1, 0:128],
                                                   axis=mybir.AxisListType.X, op=ALU.max))
            p.raw("dve", lambda E: E.drain())
            tt(p, "dve", GMX[0:1, 0:1], GMX[0:1, 0:1], PRM[0:1, 50:51], ALU.max)
            p.raw("dve", lambda E: E.drain())
            p.step("dve")
            p.wait("sp", "m_dve", p.cnt["m_dve"])
    dvc = p.cnt["m_dve"]
    p.wait("act", "m_dve", dvc)
    st = p.dma("act", gs_d[0, 0:1], GMX[0:1, 0:1])
    p.wait("sp", "q_act", st)
    ld = p.dma("sp", BC[:, 12:13], gs_d[0, 0:1].partition_broadcast(128))
    p.wait("dve", "q_sp", ld)
    recip(p, BC[:, 13:14], BC[:, 12:13])
    p.step("dve")
    p.barrier()
    if stop_after == "r3":
        p.dma("act", w["x4o"][0:128, 0:128], GMX[:, 0:8].to_broadcast([128, 128]) if False
              else U16[:, 256:384])
        return

    K2B = p.sbuf("l3_k2b", [128, 128], BF16)
    K2T = p.sbuf("l3_k2t", [128, 128], BF16)
    VIB = p.sbuf("l3_vib", [128, 128], BF16)
    VIT = p.sbuf("l3_vit", [128, 128], BF16)
    KVB = p.sbuf("l3_kvb", [128, 128], BF16)
    KVT = p.sbuf("l3_kvt", [128, 128], BF16)
    TMP1 = p.sbuf("l3_tmp1", [128, 128], BF16)
    IC = p.sbuf("l3_ic", [128, 3, 128], BF16)
    P1T = p.sbuf("l3_p1t", [128, 2048], BF16)
    M1T = p.sbuf("l3_m1t", [128, 2048], BF16)
    R1R = p.sbuf("l3_r1r", [128, 2048], F32)
    QF = p.sbuf("l3_qf", [128, 2048], BF16)
    LM2 = p.sbuf("l3_lm2", [128, 256], BF16)
    VO = p.sbuf("l3_vo", [128, 72], BF16)
    ZB = p.sbuf("l3_zb", [128, 72], BF16)

    for jj, sc_ in ((0, 7.0), (1, 15.0), (2, 13.0)):
        ts1(p, "dve", IC[:, jj, :], ident[:], sc_, ALU.mult)
        p.raw("dve", lambda E: E.drain())
    p.raw("dve", lambda E: E.memset(VO[:, 64:65], 1.0))
    p.step("dve")

    def petrans(dst, src):
        p.wait_eng("pe", "dve", p.cnt["m_dve"])
        p.wait_eng("pe", "act", p.cnt["m_act"])
        p.raw("pe", lambda E, d=dst, s=src: E.transpose(out=ps[7][:, 0:128].bitcast(BF16)[:, 0:128],
                                                        in_=s, identity=ident[:]))
        pec = p.step("pe")
        p.wait_eng("act", "pe", pec)
        act(p, dst, ps[7][:, 0:128].bitcast(BF16)[:, 0:128], AF.Copy, scale=1.0, bias=0.0)
        avc = p.step("act")
        p.wait_eng("dve", "act", avc)
        p.wait_eng("pe", "act", avc)

    for b in range(2):
        tb = b * 2048
        for h in range(16):
            oc, r0 = h // 2, (h % 2) * 64
            lk = p.dma("sp", K2B[:, :], k2_d[b * 16 + h])
            p.wait("dve", "q_sp", lk)
            p.wait("pe", "q_sp", lk)
            petrans(K2T[:, :], K2B[:, :])
            ts1(p, "dve", VIB[:, :], K2T[:, :], BC[:, 13:14], ALU.mult)
            dvc = p.step("dve")
            for it in range(6):
                p.wait_eng("pe", "dve", p.cnt["m_dve"])
                mm(p, ps[0][:, 0:128], K2T[:, :], VIB[:, :], start=True, stop=True)
                pec = p.step("pe")
                p.wait_eng("act", "pe", pec)
                act(p, KVB[:, :], ps[0][:, 0:128].bitcast(BF16)[:, 0:128] if False
                    else ps[0][:, 0:128], AF.Copy, scale=1.0, bias=0.0)
                avc = p.step("act")
                p.wait_eng("dve", "act", avc)
                petrans(KVT[:, :], KVB[:, :])
                ts1(p, "dve", TMP1[:, :], KVB[:, :], -1.0, ALU.mult)
                tt(p, "dve", TMP1[:, :], TMP1[:, :], IC[:, 0, :], ALU.add)
                dvc = p.step("dve")
                p.wait_eng("pe", "dve", dvc)
                mm(p, ps[1][:, 0:128], KVT[:, :], TMP1[:, :], start=True, stop=True)
                pec = p.step("pe")
                p.wait_eng("act", "pe", pec)
                act(p, TMP1[:, :], ps[1][:, 0:128], AF.Copy, scale=-1.0, bias=0.0)
                avc = p.step("act")
                p.wait_eng("dve", "act", avc)
                tt(p, "dve", TMP1[:, :], TMP1[:, :], IC[:, 1, :], ALU.add)
                dvc = p.step("dve")
                p.wait_eng("pe", "dve", dvc)
                mm(p, ps[2][:, 0:128], KVT[:, :], TMP1[:, :], start=True, stop=True)
                pec = p.step("pe")
                p.wait_eng("act", "pe", pec)
                act(p, TMP1[:, :], ps[2][:, 0:128], AF.Copy, scale=-1.0, bias=0.0)
                avc = p.step("act")
                p.wait_eng("dve", "act", avc)
                tt(p, "dve", TMP1[:, :], TMP1[:, :], IC[:, 2, :], ALU.add)
                dvc = p.step("dve")
                petrans(VIT[:, :], VIB[:, :])
                p.wait_eng("pe", "dve", dvc)
                mm(p, ps[3][:, 0:128], VIT[:, :], TMP1[:, :], start=True, stop=True)
                pec = p.step("pe")
                p.wait_eng("act", "pe", pec)
                act(p, VIB[:, :], ps[3][:, 0:128], AF.Copy, scale=0.25, bias=0.0)
                avc = p.step("act")
                p.wait_eng("dve", "act", avc)
                p.wait_eng("pe", "act", avc)
            p.wait("sp", "m_pe", p.cnt["m_pe"])
            lq = p.dma("sp", QF[0:64, :], Qd[oc][r0:r0 + 64, tb:tb + 2048])
            lm = p.dma("sp", LM2[0:64, 0:128], klm_d[oc][r0:r0 + 64, b * 128:(b + 1) * 128])
            lm = p.dma("sp", LM2[0:64, 128:256], qlm_d[oc][r0:r0 + 64, b * 128:(b + 1) * 128])
            p.wait("pe", "q_sp", lm)
            for lt in range(4):
                sl = slice(lt * 512, (lt + 1) * 512)
                p.wait_eng("pe", "act", p.cnt["m_act"])
                p.wait_eng("pe", "dve", p.cnt["m_dve"])
                mm(p, ps[0][:], LM2[0:64, 0:128], QF[0:64, sl], start=True, stop=True)
                pec = p.step("pe")
                p.wait_eng("act", "pe", pec)
                act(p, P1T[:, sl], ps[0][:], AF.Exp)
                avc = p.step("act")
                p.wait_eng("pe", "act", avc)
                mm(p, ps[1][:], ones128[:], P1T[:, sl], start=True, stop=True)
                pec = p.step("pe")
                p.wait_eng("act", "pe", pec)
                act(p, SCR[:, 0:512], ps[1][:], AF.Copy, scale=1.0, bias=0.0)
                avc = p.step("act")
                p.wait_eng("dve", "act", avc)
                recip(p, R1R[:, sl], SCR[:, 0:512])
                dvc = p.step("dve")
                p.wait_eng("pe", "dve", dvc)
                mm(p, ps[2][:], VIB[:, :], P1T[:, sl], start=True, stop=True)
                pec = p.step("pe")
                p.wait_eng("act", "pe", pec)
                act(p, SCR[:, 0:512], ps[2][:], AF.Copy, scale=1.0, bias=0.0)
                avc = p.step("act")
                p.wait_eng("dve", "act", avc)
                tt(p, "dve", M1T[:, sl], SCR[:, 0:512], R1R[:, sl], ALU.mult)
                dvc = p.step("dve")
                p.wait_eng("pe", "dve", dvc)
            for lc in range(16):
                p.wait("sp", "m_pe", p.cnt["m_pe"])
                lkc = p.dma("sp", QF[0:64, 0:128] if False else K2B[0:64, 0:128],
                            Kd[oc][r0:r0 + 64, tb + lc * 128:tb + (lc + 1) * 128])
                lvv = p.dma("sp", VO[:, 0:64],
                            vtm_d[tb + lc * 128:tb + (lc + 1) * 128, h * 64:(h + 1) * 64])
                p.wait("pe", "q_sp", lvv)
                p.wait_eng("pe", "act", p.cnt["m_act"])
                mm(p, ps[3][:, 0:128], K2B[0:64, 0:128], LM2[0:64, 128:256], start=True, stop=True)
                pec = p.step("pe")
                p.wait_eng("act", "pe", pec)
                act(p, K2T[:, :], ps[3][:, 0:128], AF.Exp)
                avc = p.step("act")
                p.wait_eng("pe", "act", avc)
                mm(p, ps[4][:, 0:72][:, 0:65], K2T[:, :], VO[:, 0:65],
                   start=(lc == 0), stop=(lc == 15))
                p.step("pe")
            pec = p.cnt["m_pe"]
            p.wait_eng("act", "pe", pec)
            act(p, SCR[:, 0:65], ps[4][:, 0:72][:, 0:65], AF.Copy, scale=1.0, bias=0.0)
            avc = p.step("act")
            p.wait_eng("dve", "act", avc)
            recip(p, PRM[:, 48:49], SCR[:, 64:65])
            p.raw("dve", lambda E: E.drain())
            ts1(p, "dve", ZB[:, 0:64], SCR[:, 0:64], PRM[:, 48:49], ALU.mult)
            dvc = p.step("dve")
            p.wait_eng("pe", "dve", dvc)
            for lt in range(4):
                sl = slice(lt * 512, (lt + 1) * 512)
                p.wait_eng("pe", "act", p.cnt["m_act"])
                mm(p, ps[5][0:64, :], ZB[:, 0:64], M1T[:, sl], start=True, stop=True)
                pec = p.step("pe")
                p.wait_eng("act", "pe", pec)
                p.wait("act", "q_act", max(0, p.cnt["q_act"] - 16))
                act(p, QT[0:64, (lt % 2) * 512:(lt % 2 + 1) * 512], ps[5][0:64, :],
                    AF.Copy, scale=1.0, bias=0.0)
                p.step("act")
                p.dma("act", xat_d[oc][r0:r0 + 64, tb + lt * 512:tb + (lt + 1) * 512],
                      QT[0:64, (lt % 2) * 512:(lt % 2 + 1) * 512])
            p.wait("sp", "q_act", p.cnt["q_act"])
    p.barrier()
    if stop_after == "r4":
        xob = w["x4o"].bitcast(BF16)
        for c in range(NCH):
            ld = p.dma("sp", U16[:, 0:4096], xat_d[c])
            p.wait("act", "q_sp", ld)
            p.dma("act", xob[c * 128:(c + 1) * 128, 0:4096], U16[:, 0:4096])
            p.wait("sp", "q_act", p.cnt["q_act"])
        return

    HQ = 2816
    NHC = HQ // 128
    wgT_d = p.dram("l3_wgT", [1024, HQ], BF16)
    wuT_d = p.dram("l3_wuT", [1024, HQ], BF16)
    wdT_d = p.dram("l3_wdT", [HQ, 1024], BF16)
    hq_d = p.dram("l3_hq", [NHC, 128, NT], BF16)
    T1s, T2s = U32[:, 0:1024], U32[:, 1024:2048]
    ABs, ACs = U32[:, 2048:3072], U32[:, 3072:4096]
    AC2s, SRCs = U32[:, 4096:5120], U32[:, 5120:6144]
    BSCs = U32[:, 6144:6656]

    for src_w, dst_T in ((w["l3_wg"], wgT_d), (w["l3_wu"], wuT_d)):
        for r in range(NHC):
            ldx = p.dma("sp", SRCs, src_w[r * 128:(r + 1) * 128, :])
            p.wait("dve", "q_sp", ldx)
            p.wait("dve", "q_sp", p.cnt["q_sp"])
            fp4q(p, QT[:, 0:1024], SRCs, 1024, T1s, T2s, ABs, ACs, AC2s, BSCs)
            dvq = p.step("dve")
            p.wait("sp", "m_dve", dvq)
            for j in range(8):
                par = j % 2
                p.wait("sp", "q_act", p.cnt["q_act"])
                ldt = p.dma("sp", QT2[:, par * 128:(par + 1) * 128],
                            QT[:, j * 128:(j + 1) * 128], transpose=True)
                p.wait("act", "q_sp", ldt)
                p.dma("act", dst_T[j * 128:(j + 1) * 128, r * 128:(r + 1) * 128],
                      QT2[:, par * 128:(par + 1) * 128])
            p.wait("sp", "q_act", p.cnt["q_act"])
    for r in range(8):
        ldx = p.dma("sp", U32[:, 0:2816], w["l3_wd"][r * 128:(r + 1) * 128, :])
        p.wait("dve", "q_sp", ldx)
        p.wait("dve", "q_sp", p.cnt["q_sp"])
        for qq in range(4):
            fp4q(p, U16[:, qq * 704:(qq + 1) * 704], U32[:, qq * 704:(qq + 1) * 704], 704,
                 U32[:, 2816:3520], U32[:, 3520:4224], U32[:, 4224:4928],
                 U32[:, 4928:5632], U32[:, 5632:6336], U32[:, 6336:6848])
        dvq = p.step("dve")
        p.wait("sp", "m_dve", dvq)
        for j in range(NHC):
            par = j % 2
            p.wait("sp", "q_act", p.cnt["q_act"])
            ldt = p.dma("sp", QT2[:, par * 128:(par + 1) * 128],
                        U16[:, j * 128:(j + 1) * 128], transpose=True)
            p.wait("act", "q_sp", ldt)
            p.dma("act", wdT_d[j * 128:(j + 1) * 128, r * 128:(r + 1) * 128],
                  QT2[:, par * 128:(par + 1) * 128])
        p.wait("sp", "q_act", p.cnt["q_act"])
    p.barrier()

    for oc in range(NCH):
        for j in range(32):
            lt_ = p.dma("sp", U16[:, 0:128], xn3_d[oc][:, j * 128:(j + 1) * 128])
            p.wait("pe", "q_sp", lt_)
            p.wait_eng("pe", "dve", p.cnt["m_dve"])
            p.wait_eng("pe", "act", p.cnt["m_act"])
            p.raw("pe", lambda E: E.transpose(out=ps[7][:, 0:128].bitcast(BF16)[:, 0:128],
                                              in_=U16[:, 0:128], identity=ident[:]))
            pec = p.step("pe")
            p.wait_eng("act", "pe", pec)
            act(p, U32[:, 0:128], ps[7][:, 0:128].bitcast(BF16)[:, 0:128], AF.Copy,
                scale=1.0, bias=0.0)
            avc = p.step("act")
            p.wait_eng("dve", "act", avc)
            fp4q(p, U16[:, 128:256], U32[:, 0:128], 128,
                 U32[:, 1024:1152], U32[:, 1152:1280], U32[:, 1280:1408],
                 U32[:, 1408:1536], U32[:, 1536:1664], U32[:, 1664:1792])
            dvq = p.step("dve")
            p.wait_eng("pe", "dve", dvq)
            p.raw("pe", lambda E: E.transpose(out=ps[7][:, 0:128].bitcast(BF16)[:, 0:128],
                                              in_=U16[:, 128:256], identity=ident[:]))
            pec = p.step("pe")
            p.wait_eng("act", "pe", pec)
            act(p, xn[:, oc, j * 128:(j + 1) * 128], ps[7][:, 0:128].bitcast(BF16)[:, 0:128],
                AF.Copy, scale=1.0, bias=0.0)
            avc = p.step("act")
            p.wait_eng("pe", "act", avc)
            p.wait("sp", "m_act", avc)
    p.barrier()
    if stop_after == "r5b":
        xob = w["x4o"].bitcast(BF16)
        for c in range(NCH):
            p.dma("act", xob[c * 128:(c + 1) * 128, 0:4096], xn[:, c, :])
        return

    WGB = WQB
    for Hc in range(NHC):
        p.wait("sp", "m_pe", p.cnt["m_pe"])
        lg_ = p.dma("sp", U16[:, 0:1024],
                    wgT_d[:, Hc * 128:(Hc + 1) * 128].rearrange("(i p) n -> p i n", p=128))
        lu_ = p.dma("sp", U16[:, 1024:2048],
                    wuT_d[:, Hc * 128:(Hc + 1) * 128].rearrange("(i p) n -> p i n", p=128))
        p.wait("pe", "q_sp", lu_)
        gb_ = U16[:, 0:1024].rearrange("p (i n) -> p i n", n=128)
        ub_ = U16[:, 1024:2048].rearrange("p (i n) -> p i n", n=128)
        for tt8 in range(NTT):
            ts = slice(tt8 * 512, (tt8 + 1) * 512)
            p.wait_eng("pe", "act", p.cnt["m_act"])
            p.wait_eng("pe", "dve", p.cnt["m_dve"])
            for i in range(NCH):
                mm(p, ps[0][:], gb_[:, i, :], xn[:, i, ts], start=(i == 0), stop=(i == NCH - 1))
            for i in range(NCH):
                mm(p, ps[1][:], ub_[:, i, :], xn[:, i, ts], start=(i == 0), stop=(i == NCH - 1))
            pec = p.step("pe")
            p.wait_eng("act", "pe", pec)
            act(p, SCR[:, 0:512], ps[0][:], AF.Silu)
            p.raw("act", lambda E: E.drain())
            act(p, SM[:, 0:512], ps[1][:], AF.Copy, scale=1.0, bias=0.0)
            avc = p.step("act")
            p.wait_eng("dve", "act", avc)
            tt(p, "dve", U32[:, 2048:2560], SCR[:, 0:512], SM[:, 0:512], ALU.mult)
            dvh = p.step("dve")
            for j4 in range(4):
                p.wait_eng("pe", "dve", p.cnt["m_dve"])
                p.wait_eng("pe", "act", p.cnt["m_act"])
                p.raw("pe", lambda E, j4=j4: E.transpose(
                    out=ps[7][:, 0:128], in_=U32[:, 2048 + j4 * 128:2048 + (j4 + 1) * 128],
                    identity=identf[:]))
                pec = p.step("pe")
                p.wait_eng("act", "pe", pec)
                act(p, U32[:, 0:128], ps[7][:, 0:128], AF.Copy, scale=1.0, bias=0.0)
                avc = p.step("act")
                p.wait_eng("dve", "act", avc)
                fp4q(p, U16[:, 2048 + j4 * 128:2048 + (j4 + 1) * 128], U32[:, 0:128], 128,
                     U32[:, 1024:1152], U32[:, 1152:1280], U32[:, 1280:1408],
                     U32[:, 1408:1536], U32[:, 1536:1664], U32[:, 1664:1792])
                dvq = p.step("dve")
                p.wait_eng("pe", "dve", dvq)
                p.raw("pe", lambda E, j4=j4: E.transpose(
                    out=ps[7][:, 0:128].bitcast(BF16)[:, 0:128],
                    in_=U16[:, 2048 + j4 * 128:2048 + (j4 + 1) * 128], identity=ident[:]))
                pec = p.step("pe")
                p.wait_eng("act", "pe", pec)
                p.wait("act", "q_act", max(0, p.cnt["q_act"] - 16))
                act(p, U16[:, 3072 + j4 * 128:3072 + (j4 + 1) * 128],
                    ps[7][:, 0:128].bitcast(BF16)[:, 0:128], AF.Copy, scale=1.0, bias=0.0)
                avc = p.step("act")
                p.wait_eng("pe", "act", avc)
            p.dma("act", hq_d[Hc][:, ts], U16[:, 3072:3584])
            p.wait("sp", "q_act", p.cnt["q_act"])
            p.wait("dve", "q_act", p.cnt["q_act"])
    p.barrier()

    for oc in range(NCH):
        p.wait("sp", "m_pe", p.cnt["m_pe"])
        p.wait("sp", "m_dve", p.cnt["m_dve"])
        lw = p.dma("sp", xn[:, 0, 0:2816].rearrange("p (i n) -> p i n", n=128),
                   wdT_d[:, oc * 128:(oc + 1) * 128].rearrange("(i p) n -> p i n", p=128))
        p.wait("pe", "q_sp", lw)
        wdb = xn[:, 0, 0:2816].rearrange("p (i n) -> p i n", n=128)
        for tt8 in range(NTT):
            ts = slice(tt8 * 512, (tt8 + 1) * 512)
            pa = ps[4 + (tt8 % 2)]
            p.wait_eng("pe", "act", p.cnt["m_act"])
            for half in range(2):
                i0, i1 = (0, 11) if half == 0 else (11, NHC)
                p.wait("sp", "m_pe", p.cnt["m_pe"])
                for i in range(i0, i1):
                    p.dma("sp", U16[:, (i - i0) * 512:(i - i0 + 1) * 512], hq_d[i][:, ts])
                lh = p.cnt["q_sp"]
                p.wait("pe", "q_sp", lh)
                for i in range(i0, i1):
                    mm(p, pa[:], wdb[:, i, :], U16[:, (i - i0) * 512:(i - i0 + 1) * 512],
                       start=(i == 0), stop=(i == NHC - 1))
                pec = p.step("pe")
            p.wait_eng("act", "pe", pec)
            p.wait_eng("act", "dve", p.cnt["m_dve"])
            act(p, SCR[:, 0:512], pa[:], AF.Copy, scale=1.0, bias=0.0)
            avc = p.step("act")
            p.wait_eng("dve", "act", avc)
            lx = p.dma("sp", U32[:, 2048:2560], x3f[oc * 128:(oc + 1) * 128, ts])
            la = p.dma("sp", U16[:, 6144:6656], xat_d[oc][:, ts])
            p.wait("dve", "q_sp", la)
            tt(p, "dve", U32[:, 2560:3072], SCR[:, 0:512], U32[:, 2048:2560], ALU.add)
            p.raw("dve", lambda E: E.drain())
            tt(p, "dve", U32[:, 2560:3072], U32[:, 2560:3072], U16[:, 6144:6656], ALU.add)
            dvc = p.step("dve")
            p.wait("act", "m_dve", dvc)
            p.dma("act", w["x4o"][oc * 128:(oc + 1) * 128, ts], U32[:, 2560:3072])
            p.wait("sp", "q_act", p.cnt["q_act"])
            p.wait("dve", "q_act", p.cnt["q_act"])
    p.barrier()


def l3_host_inputs(inp):
    d = {}
    d["l3_WqT"] = np.ascontiguousarray(np.asarray(inp["L3_attn_Wq"], np.float32).T)
    d["l3_WkT"] = np.ascontiguousarray(np.asarray(inp["L3_attn_Wk"], np.float32).T)
    d["l3_WvT"] = np.ascontiguousarray(np.asarray(inp["L3_attn_Wv"], np.float32).T)
    d["l3_bqc"] = col_pack(inp["L3_attn_bq"], 8)
    d["l3_bkc"] = col_pack(inp["L3_attn_bk"], 8)
    d["l3_bvc"] = col_pack(inp["L3_attn_bv"], 8)
    d["l3_wg"] = np.asarray(inp["L3_ffn_w_gate"], np.float32)
    d["l3_wu"] = np.asarray(inp["L3_ffn_w_up"], np.float32)
    d["l3_wd"] = np.asarray(inp["L3_ffn_w_down"], np.float32)
    return d


def build_launch_l3(stop_after=None):
    p = Prog("l3")
    w = {}
    w["x3f"] = p.inp("x3f", [D, NTG], F32)
    for name, shape, dt in [
        ("l3_WqT", [1024, 1024], F32), ("l3_WkT", [1024, 1024], F32),
        ("l3_WvT", [1024, 1024], F32),
        ("l3_bqc", [128, 8], F32), ("l3_bkc", [128, 8], F32), ("l3_bvc", [128, 8], F32),
        ("l3_wg", [2816, 1024], F32), ("l3_wu", [2816, 1024], F32),
        ("l3_wd", [1024, 2816], F32),
        ("cvec", [128, 8], F32), ("ones128", [128, 128], BF16),
        ("ident", [128, 128], BF16), ("identf", [128, 128], F32),
    ]:
        w[name] = p.inp(name, shape, dt)
    w["x4o"] = p.outp("x4o", [D, NT], F32)
    emit_l3(p, w, stop_after)
    p.finalize()
    return p
